# version 18
# baseline (speedup 1.0000x reference)
import math
import numpy as np
import ml_dtypes
import concourse.bass as bass
import concourse.mybir as mybir
from concourse.bass_utils import run_bass_kernel_spmd

F32 = mybir.dt.float32
BF16 = mybir.dt.bfloat16
AF = mybir.ActivationFunctionType
ALU = mybir.AluOpType
AX = mybir.AxisListType

B, S, D = 4, 2048, 4096
P = 128
NFC = 32
NKVH = 8
SCALE = 1.0 / math.sqrt(128.0)
EPS = 1e-5
NEG = -30000.0

T_SETS = ([0, 3, 4, 7, 8, 11, 12, 15], [1, 2, 5, 6, 9, 10, 13, 14])
OWN_ROW = [0, 1, 2, 3, 8, 9, 10, 11]


def col_order(p):
    own, oth = T_SETS[p], T_SETS[1 - p]
    return own[0:4] + oth[0:4] + own[4:8] + oth[4:8]


def attn_structure():
    cols = {p: col_order(p) for p in (0, 1)}
    slots = []
    mask_index = {}
    mi = 0
    for s in range(8):
        cq = s if s < 4 else s + 4
        blocks = []
        for sec in range(4):
            base = sec * 4
            pref = 0
            for p in (0, 1):
                gq = cols[p][cq]
                n = sum(1 for k in range(4) if cols[p][base + k] <= gq)
                pref = max(pref, n)
            blocks += [base + k for k in range(pref)]
        masked = [c for c in blocks
                  if any(cols[p][c] >= cols[p][cq] for p in (0, 1))]
        for c in masked:
            mask_index[(s, c)] = mi
            mi += 1
        slots.append((blocks, masked))
    return cols, slots, mask_index, mi


COLS, SLOTS, _MASK_INDEX_FULL, _NMASK_FULL = attn_structure()
_PAIRS = {}
MASK_INDEX = {}
for _s in range(8):
    _cq = _s if _s < 4 else _s + 4
    for _c in SLOTS[_s][1]:
        _cls = tuple(
            ("tri" if COLS[_p][_c] == COLS[_p][_cq] else
             ("neg" if COLS[_p][_c] > COLS[_p][_cq] else "zero"))
            for _p in (0, 1))
        if _cls not in _PAIRS:
            _PAIRS[_cls] = len(_PAIRS)
        MASK_INDEX[(_s, _c)] = _PAIRS[_cls]
NMASK = len(_PAIRS)

ENGS = ("pe", "dve", "act", "pool", "sync")
SEM_ENG = {"spe": "pe", "sdve": "dve", "sact": "act", "spool": "pool"}


class Sched:
    def __init__(self):
        self.streams = {e: [] for e in ENGS}
        self.cnt = {}
        self.waited = {e: {} for e in ENGS}
        self.sems = {}
        self.guards = []

    def new_sem(self, nc, name):
        g = nc.semaphore(name)
        self.sems[name] = g.__enter__()
        self.guards.append(g)
        self.cnt[name] = 0

    def wait(self, eng, tok):
        if tok is None:
            return
        name, val = tok
        w = self.waited[eng]
        if w.get(name, 0) >= val:
            return
        w[name] = val
        h = self.sems[name]
        self.streams[eng].append(lambda e, h=h, v=val: e.wait_ge(h, v))

    def emit(self, eng, fn, waits=(), inc=None, amt=1):
        for t in waits:
            self.wait(eng, t)
        if inc is None:
            self.streams[eng].append(fn)
            return None
        self.cnt[inc] += amt
        tok = (inc, self.cnt[inc])
        h = self.sems[inc]
        self.streams[eng].append(
            lambda e, fn=fn, h=h, a=amt: fn(e).then_inc(h, a))
        return tok

    def chain(self, eng):
        sem = {"dve": "sdve", "act": "sact", "pool": "spool",
               "pe": "spe"}[eng]
        state = {"tok": None}

        def em(fn, waits=()):
            w = list(waits)
            if state["tok"] is not None:
                w.append(state["tok"])
            t = self.emit(eng, fn, waits=w, inc=sem)
            state["tok"] = t
            return t
        return em

    def fence(self):
        toks = []
        fs = self.fence_scratch
        ident = self.fence_ident
        ops = {"pe": lambda en: en.ldweights(ident[:]),
               "dve": lambda en: en.memset(fs[0:1, 0:1], 0.0),
               "act": lambda en: en.activation(fs[0:1, 0:1], fs[0:1, 1:2],
                                               AF.Identity, bias=0.0,
                                               scale=0.0),
               "pool": lambda en: en.memset(fs[0:1, 1:2], 0.0)}
        for e in ("pe", "dve", "act", "pool"):
            sem = "f_" + e
            self.cnt[sem] += 1
            toks.append((sem, self.cnt[sem]))
            h = self.sems[sem]
            op = ops[e]
            self.streams[e].append(lambda en, h=h, op=op: op(en).then_inc(h, 1))
        toks.append(("sx", self.cnt["sx"]))
        toks.append(("sw", self.cnt["sw"]))
        for e in ENGS:
            for t in toks:
                self.wait(e, t)


class Buf:
    def __init__(self):
        self.w = {}
        self.r = {}

    def wait_write(self):
        return list(self.w.items()) + list(self.r.items())

    def wait_read(self):
        return list(self.w.items())

    def set_written(self, tok):
        if tok is None:
            return
        s, v = tok
        self.w[s] = max(self.w.get(s, 0), v)

    def add_reader(self, tok):
        if tok is None:
            return
        s, v = tok
        self.r[s] = max(self.r.get(s, 0), v)


class KB:
    def __init__(self, stop_after=None, taps=(), attn_units=None,
                 attn_stage=99):
        self.stop_after = stop_after
        self.tap_list = list(taps)
        self.attn_units = attn_units
        self.attn_stage = attn_stage
        self._unit_count = 0
        self._pend = None
        self.nc = bass.Bass("TRN2", target_bir_lowering=False,
                            num_swdge_queues=2,
                            detect_race_conditions=False)
        self.sch = Sched()
        self.bufs = {}
        self.guards = []
        self.pool_dma_toks = []

    def buf(self, name):
        if name not in self.bufs:
            self.bufs[name] = Buf()
        return self.bufs[name]

    def sb(self, name, shape, dt):
        g = self.nc.sbuf_tensor(name, shape, dt)
        t = g.__enter__()
        self.guards.append(g)
        return t

    def pool_dma(self, dst, src, waits=()):
        w = list(waits)
        if len(self.pool_dma_toks) >= 2:
            w.append(self.pool_dma_toks[-2])
        tok = self.sch.emit("pool", lambda e, d=dst, s=src:
                            e.dma_start(out=d, in_=s),
                            waits=w, inc="sw", amt=16)
        self.pool_dma_toks.append(tok)
        return tok

    def build(self):
        nc, sch = self.nc, self.sch
        for s in ("sx", "sw", "spe", "sact", "sdve", "spool", "sst",
                  "f_pe", "f_dve", "f_act", "f_pool"):
            sch.new_sem(nc, s)

        self.x_d = nc.dram_tensor("x_perm", [S, D], F32, kind="ExternalInput").ap()
        self.mask_d = nc.dram_tensor("masks", [P, NMASK * P], BF16,
                                     kind="ExternalInput").ap()
        self.wqkv_d = nc.dram_tensor("wqkv_p", [48, P, 4096], F32,
                                     kind="ExternalInput").ap()
        self.wd_d = nc.dram_tensor("wd_p", [8, P, 16384], F32,
                                   kind="ExternalInput").ap()
        self.wb_d = nc.dram_tensor("wb_p", [8, P, 16384], F32,
                                   kind="ExternalInput").ap()
        self.wr_d = nc.dram_tensor("wr_p", [P, 256], F32,
                                   kind="ExternalInput").ap()
        self.lA_d = nc.dram_tensor("lA_p", [P, 4096], F32,
                                   kind="ExternalInput").ap()
        self.lB_d = nc.dram_tensor("lB_p", [P, 4096], F32,
                                   kind="ExternalInput").ap()
        self.bb_d = nc.dram_tensor("bb", [1, D], F32, kind="ExternalInput").ap()
        self.br_d = nc.dram_tensor("br", [1, 8], F32, kind="ExternalInput").ap()
        self.y_d = nc.dram_tensor("y_out", [1024, D], F32,
                                  kind="ExternalOutput").ap()

        AR_F32 = 52480
        g = nc.sbuf_tensor("arena", [P, AR_F32], F32)
        self.arena = g.__enter__()
        self.guards.append(g)
        self.ar_h = self.arena.ap().tensor
        self.ar_bf = self.ar_h.bitcast(BF16)

        self.O_REGA = 0
        self.O_R64 = 32768
        self.O_KV = 98304
        self.O_D = 163840
        self.O_E = 176128

        self.xp = [self.vf(self.O_E, 512), self.vf(self.O_E + 2048, 512)]
        self.xh1_tm = self.vb(self.O_E + 4096, 2048)
        self.v_tmp = self.vb(self.O_E + 8192, 512)
        self.probs = self.vb(self.O_E + 9216, 2048)
        self.probs_n = [self.vb(self.O_E + 13312, 1024),
                        self.vb(self.O_E + 15360, 1024)]
        self.probsT_all = self.vb(self.O_E + 17408, 8192)
        self.wbuf = [self.vb(self.O_D + i * 4096, 2048) for i in range(3)]
        self.dq = [self.vb(self.O_E + i * 16384, 8192) for i in range(2)]
        self.xs2 = [self.vb(self.O_D, 2048), self.vb(self.O_D + 4096, 2048)]
        self.xh2_tm = [self.vb(self.O_E, 4096), self.vb(self.O_E + 8192, 4096)]
        self.lA_sb = self.vb(self.O_REGA, 4096)
        self.lB_sb = self.vb(self.O_REGA + 8192, 4096)
        self.bb_sb = self.vb(self.O_REGA + 16384, 4096, rows=1)
        self.wr_sb = self.vb(self.O_REGA + 24576, 256)
        self.hwT = self.vb(self.O_REGA + 25600, 1024)
        self.op_ = [self.vf(self.O_D, 512), self.vf(self.O_D + 2048, 512)]

        sb = self.sb
        self.ident = sb("ident", [P, P], BF16)
        self.masks_sb = sb("masks_sb", [P, NMASK * P], BF16)
        self.sum_part = sb("sum_part", [P, 8], F32)
        self.sumsq_part = sb("sumsq_part", [P, 8], F32)
        self.s_col = sb("s_col", [P, 1], F32)
        self.q_col = sb("q_col", [P, 1], F32)
        self.m_col = sb("m_col", [P, 1], F32)
        self.var_col = sb("var_col", [P, 1], F32)
        self.rstd_col = sb("rstd_col", [P, 1], F32)
        self.nsm = sb("nsm", [P, 2], F32)
        self.l_part = sb("l_part", [P, 2], F32)
        self.l_col = sb("l_col", [P, 2], F32)
        self.r_col = sb("r_col", [P, 2], F32)
        self.zbuf = sb("zbuf", [P, 8], F32)
        self.zs0 = sb("zs0", [P, 8], F32)
        self.zs1 = sb("zs1", [P, 8], F32)
        self.cums = sb("cums", [P, 8], F32)
        self.kz = sb("kz", [P, 4], F32)
        self.kiota = sb("kiota", [P, 8], F32)
        self.ones_col = sb("ones_col", [1, P], BF16)
        self.br_sb = sb("br_sb", [1, 8], BF16)
        self.hw = sb("hw", [P, P], BF16)
        self.eps_col = sb("eps_col", [P, 1], F32)
        self.fence_scratch = sb("fence_scratch", [1, 2], F32)
        sch.fence_scratch = self.fence_scratch
        sch.fence_ident = self.ident

        g = nc.psum_tensor("psA", [P, 4096], F32)
        self.psum = g.__enter__()
        self.guards.append(g)
        self.ps_h = self.psum.ap().tensor
        self.ps_bf = self.ps_h.bitcast(BF16)

        def attn0_bg():
            bg = self.p1_gen(1)
            self.attn(0, bg=bg)
            for _ in bg:
                pass
        phases = [("p0", lambda: self.p0()),
                  ("p1g0", lambda: self.p1(0)),
                  ("p2g0", lambda: self.p2(0)),
                  ("attn0", attn0_bg),
                  ("p1g1", lambda: None),
                  ("p2g1", lambda: self.p2(1)),
                  ("consts", lambda: self.ffn_consts()),
                  ("attn1", lambda: self.attn(1)),
                  ("fence1", lambda: sch.fence()),
                  ("dense", lambda: self.dense()),
                  ("fence2", lambda: sch.fence()),
                  ("ffn", lambda: self.ffn())]
        for name, fn in phases:
            fn()
            if self.stop_after == name:
                break
        if self.tap_list:
            sch.fence()
            for name, off, cols in self.tap_list:
                tap_d = nc.dram_tensor(name, [P, cols], F32,
                                       kind="ExternalOutput").ap()
                if isinstance(off, str):
                    src_ap = getattr(self, off)[:] if cols is None else \
                        getattr(self, off)[:, 0:cols]
                else:
                    src_ap = self.vf(off, cols)
                sch.emit("sync", lambda e, d=tap_d, s=src_ap:
                         e.dma_start(out=d, in_=s), inc="sst", amt=16)
        sch.wait("sync", ("sst", sch.cnt["sst"]))

        with nc.Block() as block:
            block.tensor(lambda e: [f(e) for f in sch.streams["pe"]])
            block.vector(lambda e: [f(e) for f in sch.streams["dve"]])
            block.scalar(lambda e: [f(e) for f in sch.streams["act"]])
            block.gpsimd(lambda e: [f(e) for f in sch.streams["pool"]])
            block.sync(lambda e: [f(e) for f in sch.streams["sync"]])

        for g in reversed(self.guards):
            g.__exit__(None, None, None)
        return nc

    def vf(self, off, cols, rows=P):
        return self.ar_h[0:rows, off // 4: off // 4 + cols]

    def vb(self, off, cols, rows=P):
        return self.ar_bf[0:rows, off // 2: off // 2 + cols]

    def psf(self, bank, cols, off=0):
        c0 = bank * 512 + off
        return self.ps_h[0:P, c0:c0 + cols]

    def psb(self, bank, cols, off=0):
        c0 = bank * 1024 + off
        return self.ps_bf[0:P, c0:c0 + cols]

    def p0(self):
        sch, buf = self.sch, self.buf
        t0 = sch.emit("pool", lambda e: e.memset(self.ident[:], 0.0),
                      inc="spool")
        t = sch.emit("pool", lambda e: e.affine_select(
            out=self.ident[:], in_=self.ident[:], pattern=[[-1, P]],
            compare_op=ALU.not_equal, fill=1.0, base=0, channel_multiplier=1),
            waits=[t0], inc="spool")
        buf("ident").set_written(t)
        t = sch.emit("sync", lambda e: e.dma_start(
            out=self.masks_sb[:], in_=self.mask_d), inc="sx", amt=16)
        buf("masks").set_written(t)
        t = sch.emit("pool", lambda e: e.memset(self.eps_col[:], EPS),
                     inc="spool")
        buf("eps").set_written(t)

    def p1(self, g):
        for _ in self.p1_gen(g):
            pass

    def p1_gen(self, g):
        sch, buf = self.sch, self.buf
        for c in range(8):
            row0 = (g * 8 + c) * P
            t2 = None
            for pc in range(8):
                bx = buf(f"xp{pc % 2}")
                tok = sch.emit("sync", lambda e, pc=pc, row0=row0: e.dma_start(
                    out=self.xp[pc % 2],
                    in_=self.x_d[row0:row0 + P, pc * 512:(pc + 1) * 512]),
                    waits=bx.wait_write(), inc="sx", amt=16)
                bx.set_written(tok)
                w1 = bx.wait_read()
                if pc == 0:
                    w1 = w1 + buf("xh1_tm").wait_write()
                sch.emit("dve", lambda e, pc=pc: e.tensor_scalar(
                    out=self.xh1_tm[:, 0:512], in0=self.xp[pc % 2],
                    scalar1=1.0, scalar2=None, op0=ALU.mult, op1=ALU.add,
                    accum_out=self.sum_part[:, pc:pc + 1]), waits=w1)
                t2 = sch.emit("dve", lambda e, pc=pc: e.scalar_tensor_tensor(
                    out=self.xh1_tm[:, 512:1024], in0=self.xp[pc % 2],
                    scalar=1.0, in1=self.xp[pc % 2],
                    op0=ALU.mult, op1=ALU.mult,
                    accum_out=self.sumsq_part[:, pc:pc + 1]),
                    inc="sdve")
                bx.add_reader(t2)
            em = sch.chain("dve")
            em(lambda e: e.reduce_sum(
                self.s_col[:], self.sum_part[:], axis=AX.X),
                waits=buf("stat").wait_write() + [t2])
            em(lambda e: e.reduce_sum(
                self.q_col[:], self.sumsq_part[:], axis=AX.X))
            em(lambda e: e.tensor_scalar(
                out=self.m_col[:], in0=self.s_col[:], scalar1=1.0 / D,
                scalar2=None, op0=ALU.mult))
            em(lambda e: e.tensor_tensor(
                out=self.s_col[:], in0=self.m_col[:], in1=self.m_col[:],
                op=ALU.mult))
            t_var = em(lambda e: e.scalar_tensor_tensor(
                out=self.var_col[:], in0=self.q_col[:], scalar=1.0 / D,
                in1=self.s_col[:], op0=ALU.mult, op1=ALU.subtract))
            t_ln = sch.emit("act", lambda e: e.activation(
                self.rstd_col[:], self.var_col[:], AF.Ln,
                bias=self.eps_col[:], scale=1.0),
                waits=[t_var] + buf("eps").wait_read(), inc="sact")
            t_rstd = sch.emit("act", lambda e: e.activation(
                self.rstd_col[:], self.rstd_col[:], AF.Exp, bias=0.0,
                scale=-0.5), waits=[t_ln], inc="sact")
            buf("stat").set_written(t_rstd)
            for half in range(2):
                t_norm = None
                for k4 in range(4):
                    pc = half * 4 + k4
                    bx = buf(f"xp{pc % 2}")
                    tok = sch.emit("sync", lambda e, pc=pc, row0=row0:
                                   e.dma_start(
                                       out=self.xp[pc % 2],
                                       in_=self.x_d[row0:row0 + P,
                                                    pc * 512:(pc + 1) * 512]),
                                   waits=bx.wait_write(), inc="sx", amt=16)
                    bx.set_written(tok)
                    w1 = bx.wait_read() + [t_rstd, t_var]
                    if k4 == 0:
                        w1 = w1 + buf("xh1_tm").wait_write()
                    t_norm = sch.emit("dve", lambda e, pc=pc, k4=k4:
                                      e.tensor_scalar(
                                          out=self.xh1_tm[:, k4 * 512:
                                                          (k4 + 1) * 512],
                                          in0=self.xp[pc % 2],
                                          scalar1=self.m_col[:],
                                          scalar2=self.rstd_col[:],
                                          op0=ALU.subtract, op1=ALU.mult),
                                      waits=w1,
                                      inc="sdve")
                    bx.add_reader(t_norm)
                buf("xh1_tm").set_written(t_norm)
                buf("stat").add_reader(t_norm)
                for bi in range(2):
                    bank = 4 + bi
                    pt = self.psb(bank, 1024)
                    bpb = buf(f"bank{bank}")
                    last = None
                    for k in range(8):
                        col = (bi * 8 + k) * P
                        last = sch.emit("pe", lambda e, col=col, k=k, pt=pt:
                                        e.transpose(
                                            pt[:, k * P:(k + 1) * P],
                                            self.xh1_tm[:, col:col + P],
                                            self.ident[:]),
                                        waits=([t_norm] + bpb.wait_write() +
                                               buf("ident").wait_read())
                                        if k == 0 else [],
                                        inc="spe" if k == 7 else None)
                    bpb.set_written(last)
                    buf("xh1_tm").add_reader(last)
                    fc0 = half * 16 + bi * 8
                    if c < 4:
                        dst = self.vb(self.O_REGA, 16384).rearrange(
                            "p (fc j) -> p fc j", j=512)[
                            :, fc0:fc0 + 8, c * P:(c + 1) * P]
                        bname = "xh1_own"
                    else:
                        dst = self.vb(self.O_R64, 32768).rearrange(
                            "p (fc j) -> p fc j", j=1024)[
                            :, fc0:fc0 + 8,
                            512 + (c - 4) * P: 512 + (c - 3) * P]
                        bname = "xh1_oth"
                    src = pt[:].rearrange("p (k j) -> p k j", j=P)
                    if bi == 0:
                        ev = sch.emit("act", lambda e, dst=dst, src=src:
                                      e.copy(dst, src),
                                      waits=[last] + buf(bname).wait_write(),
                                      inc="sact")
                    else:
                        ev = sch.emit("dve", lambda e, dst=dst, src=src:
                                      e.tensor_copy(dst, src),
                                      waits=[last] + buf(bname).wait_write(),
                                      inc="sdve")
                    bpb.add_reader(ev)
                    buf(bname).set_written(ev)
            yield c

    def p2(self, g):
        sch, buf = self.sch, self.buf
        oc_order = list(range(32, 40)) + list(range(40, 48)) + list(range(32))
        si = 0
        for oc in oc_order:
            is_q = oc < 32
            halves = []
            for h in range(2):
                wb = buf(f"wbuf{si % 3}")
                tok = self.pool_dma(
                    self.wbuf[si % 3],
                    self.wqkv_d[oc][:, h * 2048:(h + 1) * 2048],
                    waits=wb.wait_write())
                wb.set_written(tok)
                halves.append((self.wbuf[si % 3], wb))
                si += 1
            if is_q:
                banks = [6 + (oc % 2)]
                npiece = 1
            else:
                banks = [2 + 2 * (oc % 2), 3 + 2 * (oc % 2)]
                npiece = 2
            pb = [buf(f"bank{b}") for b in banks]
            last = None
            for fc in range(NFC):
                wsb, wbb = halves[fc // 16]
                wcol = (fc % 16) * P
                for piece in range(npiece):
                    if is_q or piece == 0:
                        rhs = self.vb(self.O_REGA, 16384).rearrange(
                            "p (fc j) -> p fc j", j=512)[:, fc, :]
                    else:
                        rhs = self.vb(self.O_R64, 32768).rearrange(
                            "p (fc j) -> p fc j", j=1024)[:, fc, 512:1024]
                    bank = banks[piece]
                    w = []
                    if fc == 0 and piece == 0:
                        for b_ in pb:
                            w += b_.wait_write()
                        w += buf("xh1_own").wait_read()
                        if not is_q:
                            w += buf("xh1_oth").wait_read()
                    if fc % 16 == 0 and piece == 0:
                        w = w + wbb.wait_read()
                    last = sch.emit("pe", lambda e, wsb=wsb, wcol=wcol,
                                    bank=bank, rhs=rhs, fc=fc:
                                    e.matmul(self.psf(bank, 512),
                                             wsb[:, wcol:wcol + P], rhs,
                                             start=(fc == 0),
                                             stop=(fc == NFC - 1),
                                             skip_group_check=True),
                                    waits=w,
                                    inc="spe" if (fc == NFC - 1 and
                                                  piece == npiece - 1)
                                    else None)
            for h_ in halves:
                h_[1].add_reader(last)
            buf("xh1_own").add_reader(last)
            if not is_q:
                buf("xh1_oth").add_reader(last)
            for b_ in pb:
                b_.set_written(last)
            if is_q:
                dst = self.vb(self.O_R64 + oc * 2048 + g * 1024, 512)
                ev = sch.emit("act", lambda e, dst=dst, b0=banks[0]:
                              e.activation(dst, self.psf(b0, 512), AF.Copy,
                                           bias=0.0, scale=SCALE),
                              waits=[last], inc="sact")
                pb[0].add_reader(ev)
                buf(f"q{g}").set_written(ev)
            elif oc < 40:
                kvh = oc - 32
                dst = self.vb(self.O_KV + kvh * 4096 + g * 2048, 1024)
                ev = sch.emit("act", lambda e, dst=dst, b0=banks[0]:
                              e.copy(dst, self.psf(b0, 1024)),
                              waits=[last], inc="sact")
                pb[0].add_reader(ev)
                pb[1].add_reader(ev)
                buf(f"k{g}").set_written(ev)
            else:
                kvh = oc - 40
                for hv in range(2):
                    bv = buf("v_tmp")
                    ev = sch.emit("act", lambda e, b0=banks[0], hv=hv:
                                  e.copy(self.v_tmp,
                                         self.psf(b0, 512, off=hv * 512)),
                                  waits=[last] + bv.wait_write(), inc="sact")
                    bv.set_written(ev)
                    pb[0].add_reader(ev)
                    pb[1].add_reader(ev)
                    bank = oc % 2
                    pt = self.psb(bank, 512, off=hv * 512)
                    bpb = buf(f"bank{bank}")
                    lastt = None
                    for blk in range(4):
                        lastt = sch.emit("pe", lambda e, blk=blk, pt=pt:
                                         e.transpose(
                                             pt[:, blk * P:(blk + 1) * P],
                                             self.v_tmp[:, blk * P:
                                                        (blk + 1) * P],
                                             self.ident[:]),
                                         waits=([ev] + (bpb.wait_write()
                                                        if hv == 0 else []))
                                         if blk == 0 else [],
                                         inc="spe" if blk == 3 else None)
                    bpb.set_written(lastt)
                    bv.add_reader(lastt)
                    dst = self.vb(self.O_KV + 32768, 16384).rearrange(
                        "p (t j) -> p t j", j=1024)[
                        :, g * 8 + hv * 4:g * 8 + hv * 4 + 4,
                        kvh * P:(kvh + 1) * P]
                    evt = sch.emit("dve", lambda e, dst=dst, pt=pt:
                                   e.tensor_copy(dst, pt[:].rearrange(
                                       "p (k j) -> p k j", j=P)),
                                   waits=[lastt], inc="sdve")
                    bpb.add_reader(evt)
                    buf(f"v{g}").set_written(evt)

    def attn(self, g, bg=None):
        sch, buf = self.sch, self.buf
        slots = range(4) if g == 0 else range(4, 8)
        for s in slots:
            blocks, masked = SLOTS[s]
            L = len(blocks)
            nch = (L + 7) // 8
            own_tok = (s % 4) * P + (512 if s >= 4 else 0)
            runs = []
            for jj, c in enumerate(blocks):
                if runs and c == runs[-1][1] + runs[-1][2]:
                    runs[-1][2] += 1
                else:
                    runs.append([jj, c, 1])
            for kvh in range(NKVH):
                if self.attn_units is not None and \
                        self._unit_count >= self.attn_units:
                    return
                self._unit_count += 1
                for qh in range(4):
                    h = kvh * 4 + qh
                    ii = s * 32 + kvh * 4 + qh
                    if g == 0:
                        sbase = (ii % 2) * 1024
                        bsc = [buf(f"bank{sbase // 512 + i}") for i in range(2)]
                    else:
                        sbase = 0
                        bsc = [buf(f"bank{b}") for b in range(4)]
                    wait0 = []
                    for b_ in bsc:
                        wait0 += b_.wait_write()
                    wait0 += buf(f"q{g}").wait_read()
                    wait0 += buf("k0").wait_read() + buf("k1").wait_read()
                    first_banks = set()
                    mms = []
                    for (jj0, c0, ln) in runs:
                        col = 0
                        while col < ln * P:
                            pc = jj0 * P + col
                            seg = min(512 - (pc % 512), ln * P - col)
                            bank = pc // 512
                            st = bank not in first_banks
                            first_banks.add(bank)
                            mms.append((pc, c0 * P + col, seg, st))
                            col += seg
                    last = None
                    q_ap = self.vb(self.O_R64 + (h * 1024 + own_tok) * 2, P)
                    for idx, (pc, kc, seg, st) in enumerate(mms):
                        k_ap = self.vb(self.O_KV + kvh * 4096 + kc * 2, seg)
                        last = sch.emit(
                            "pe", lambda e, pc=pc, seg=seg, st=st,
                            q_ap=q_ap, k_ap=k_ap, sbase=sbase:
                            e.matmul(self.ps_h[0:P, sbase + pc:
                                               sbase + pc + seg],
                                     q_ap, k_ap, start=st, stop=True,
                                     skip_group_check=True),
                            waits=wait0 if idx == 0 else [], inc=None)
                    assert masked
                    for mc_i, c in enumerate(masked):
                        jj = blocks.index(c)
                        mi = MASK_INDEX[(s, c)]
                        last = sch.emit(
                            "pe", lambda e, jj=jj, mi=mi, sbase=sbase:
                            e.matmul(self.ps_h[0:P, sbase + jj * P:
                                               sbase + (jj + 1) * P],
                                     self.ident[:],
                                     self.masks_sb[:, mi * P:(mi + 1) * P],
                                     start=False, stop=True,
                                     skip_group_check=True),
                            waits=buf("masks").wait_read(),
                            inc="spe" if mc_i == len(masked) - 1 else None)
                    t_sc = last
                    if getattr(self, "_pend", None) is not None:
                        self._pend()
                        self._pend = None
                    bn = buf("nsm")
                    t_b = sch.emit("dve", lambda e, L=L, ii=ii, sbase=sbase:
                                   e.reduce_max(
                                       self.nsm[:, ii % 2:ii % 2 + 1],
                                       self.ps_h[0:P, sbase:sbase + L * P],
                                       axis=AX.X, negate=True),
                                   waits=[t_sc] + bn.wait_write(),
                                   inc="sdve")
                    bn.set_written(t_b)
                    bp = buf("probs")
                    bl = buf("l_col")
                    t_es = []
                    for ch in range(nch):
                        n = min(8, L - ch * 8) * P
                        w = [t_b]
                        if ch == 0:
                            w += bp.wait_write() + bl.wait_write()
                        t_e = sch.emit("act", lambda e, ch=ch, n=n, ii=ii,
                                       sbase=sbase:
                                       e.activation(
                                           self.probs[:, ch * 1024:
                                                      ch * 1024 + n],
                                           self.ps_h[0:P, sbase + ch * 1024:
                                                     sbase + ch * 1024 + n],
                                           AF.Exp,
                                           bias=self.nsm[:, ii % 2:ii % 2 + 1],
                                           scale=1.0,
                                           accum_out=self.l_part[:, ch:ch + 1]),
                                       waits=w, inc="sact")
                        t_es.append(t_e)
                    for b_ in bsc:
                        b_.add_reader(t_es[-1])
                    bp.set_written(t_es[-1])
                    bn.add_reader(t_es[-1])
                    brc = buf("r_col")
                    if nch == 2:
                        t_l = sch.emit("dve", lambda e, ii=ii: e.tensor_tensor(
                            out=self.l_col[:, ii % 2:ii % 2 + 1],
                            in0=self.l_part[:, 0:1], in1=self.l_part[:, 1:2],
                            op=ALU.add), waits=[t_es[-1]], inc="sdve")
                        bl.set_written(t_l)
                        t_r = sch.emit("dve", lambda e, ii=ii: e.reciprocal(
                            self.r_col[:, ii % 2:ii % 2 + 1],
                            self.l_col[:, ii % 2:ii % 2 + 1]),
                            waits=[t_l] + brc.wait_write(), inc="sdve")
                        bl.add_reader(t_r)
                    else:
                        t_r = sch.emit("dve", lambda e, ii=ii: e.reciprocal(
                            self.r_col[:, ii % 2:ii % 2 + 1],
                            self.l_part[:, 0:1]),
                            waits=[t_es[-1]] + brc.wait_write(), inc="sdve")
                        bl.set_written(t_es[-1])
                        bl.add_reader(t_r)
                    brc.set_written(t_r)
                    pn_toks = []
                    for ch in range(nch):
                        nblk = min(8, L - ch * 8)
                        bpn = buf(f"probs_n{ch}")
                        t_pn = sch.emit("dve", lambda e, ch=ch, nblk=nblk,
                                        ii=ii: e.tensor_scalar(
                                            out=self.probs_n[ch][:, 0:nblk * P],
                                            in0=self.probs[
                                                :, ch * 1024:
                                                ch * 1024 + nblk * P],
                                            scalar1=self.r_col[
                                                :, ii % 2:ii % 2 + 1],
                                            scalar2=None, op0=ALU.mult),
                                        waits=[t_r] + bpn.wait_write(),
                                        inc="sdve")
                        bpn.set_written(t_pn)
                        brc.add_reader(t_pn)
                        bp.add_reader(t_pn)
                        pn_toks.append(t_pn)
                    def deferred(qh=qh, L=L, nch=nch, pn_toks=tuple(pn_toks)):
                        for ch in range(nch):
                            nblk = min(8, L - ch * 8)
                            bank = 4 + (ch % 2)
                            pt = self.psb(bank, 1024)
                            bpb = buf(f"bank{bank}")
                            bpn = buf(f"probs_n{ch}")
                            lastt = None
                            for k in range(nblk):
                                lastt = sch.emit(
                                    "pe", lambda e, k=k, pt=pt, ch=ch:
                                    e.transpose(
                                        pt[:, k * P:(k + 1) * P],
                                        self.probs_n[ch][:, k * P:(k + 1) * P],
                                        self.ident[:]),
                                    waits=([pn_toks[ch]] + bpb.wait_write())
                                    if k == 0 else [],
                                    inc="spe" if k == nblk - 1 else None)
                            bpb.set_written(lastt)
                            bpn.add_reader(lastt)
                            bpt = buf("probsT")
                            dst = self.probsT_all[:].rearrange(
                                "p (jj x) -> p jj x", x=512)[
                                :, ch * 8:ch * 8 + nblk,
                                qh * P:(qh + 1) * P]
                            ev = sch.emit("act", lambda e, dst=dst, pt=pt,
                                          nblk=nblk:
                                          e.copy(dst, pt[:, 0:nblk * P]
                                                 .rearrange(
                                                     "p (k x) -> p k x", x=P)),
                                          waits=[lastt] +
                                          (bpt.wait_write()
                                           if ch == 0 and qh == 0 else []),
                                          inc="sact")
                            bpb.add_reader(ev)
                            bpt.set_written(ev)
                    self._pend = deferred
                if self._pend is not None:
                    self._pend()
                    self._pend = None
                av_bank = 6 + ((s * 8 + kvh) % 2)
                bav = buf(f"bank{av_bank}")
                w0 = bav.wait_write() + buf(f"v{g}").wait_read() + \
                    buf("probsT").wait_read()
                if g == 1:
                    w0 += buf("v0").wait_read()
                last = None
                first = True
                nb = len(blocks)
                for jj, c in enumerate(blocks):
                    v_ap = self.vb(self.O_KV + 32768 + c * 2048 + kvh * 256, P)
                    is_last = (jj == nb - 1)
                    last = sch.emit(
                        "pe", lambda e, jj=jj, v_ap=v_ap,
                        av_bank=av_bank, first=first, is_last=is_last:
                        e.matmul(
                            self.ps_h[0:P, av_bank * 512:
                                      av_bank * 512 + 512],
                            v_ap,
                            self.probsT_all[:, jj * 512:(jj + 1) * 512],
                            start=first, stop=is_last,
                            skip_group_check=True),
                        waits=w0 if first else [],
                        inc="spe" if is_last else None)
                    first = False
                t_av = last
                bav.set_written(t_av)
                buf("probsT").add_reader(t_av)
                dst = self.vb(self.O_R64, 32768).rearrange(
                    "p (hh j) -> p hh j", j=1024)[
                    :, kvh * 4:(kvh + 1) * 4, own_tok:own_tok + P]
                ev = sch.emit("dve", lambda e, dst=dst, av_bank=av_bank:
                              e.tensor_copy(
                                  dst,
                                  self.ps_h[0:P, av_bank * 512:
                                            av_bank * 512 + 512].rearrange(
                                      "p (q j) -> p q j", j=P)),
                              waits=[t_av], inc="sdve")
                bav.add_reader(ev)
                buf(f"attn_o{g}").set_written(ev)
                if bg is not None and (s * 8 + kvh) % 4 == 3:
                    next(bg, None)

    def dense(self):
        sch, buf = self.sch, self.buf
        for og in range(8):
            qwbs = []
            for q in range(2):
                wb = buf(f"dq{q}")
                tok = self.pool_dma(
                    self.dq[q],
                    self.wd_d[og][:, q * 8192:(q + 1) * 8192],
                    waits=wb.wait_write())
                wb.set_written(tok)
                qwbs.append(wb)
            pbs = [buf(f"bank{7 - b}") for b in range(8)]
            last_by_o = [None] * 8
            for fc in range(NFC):
                q = fc // 16
                wb = qwbs[q]
                wcol = (fc % 16) * 512
                for o in range(8):
                    w = []
                    if fc == 0:
                        w += pbs[o].wait_write()
                        w += buf("attn_o0").wait_read() + \
                            buf("attn_o1").wait_read()
                    if fc % 16 == 0:
                        w += wb.wait_read()
                    lhs = self.vb(self.O_R64 + fc * 2048 + o * 256, P)
                    last_by_o[o] = sch.emit(
                        "pe", lambda e, fc=fc, o=o, q=q, wcol=wcol, lhs=lhs:
                        e.matmul(self.psf(7 - o, 512), lhs,
                                 self.dq[q][:, wcol:wcol + 512],
                                 start=(fc == 0), stop=(fc == NFC - 1),
                                 skip_group_check=True),
                        waits=w,
                        inc="spe" if (fc == NFC - 1 or
                                      (fc % 16 == 15 and o == 7)) else None)
                if fc % 16 == 15:
                    wb.add_reader(last_by_o[7])
            xtoks = []
            for gi in range(2):
                bx = buf(f"xs2_{gi}")
                srcv = bass.AP(self.x_d.tensor,
                               gi * 1024 * D + og * 512,
                               [[D, P], [P * D, 4], [1, 512]])
                tokx = self.pool_dma(self.xs2[gi], srcv,
                                     waits=bx.wait_write())
                bx.set_written(tokx)
                xtoks.append((bx, tokx))
            for o in range(8):
                gi, oo = o // 4, o % 4
                bx, tokx = xtoks[gi]
                t_mm = sch.emit("pe", lambda e, o=o, gi=gi, oo=oo: e.matmul(
                    self.psf(7 - o, 512), self.ident[:],
                    self.xs2[gi][:, oo * 512:(oo + 1) * 512],
                    start=False, stop=True, skip_group_check=True),
                    waits=[last_by_o[o], tokx], inc="spe")
                bx.add_reader(t_mm)
                dst = self.vb(self.O_KV + o * 8192 + og * 1024, 512)
                t_ev = sch.emit("dve", lambda e, o=o, dst=dst:
                                e.tensor_copy(dst, self.psf(7 - o, 512)),
                                waits=[t_mm], inc="sdve")
                pbs[o].add_reader(t_ev)
                buf("x1").set_written(t_ev)
        if self.stop_after == "dense_mm":
            return
        sch.fence()
        pend = [None]
        for o in range(8):
            pb_ = o % 2
            x1v = self.vb(self.O_KV + o * 8192, 4096)
            em = sch.chain("dve")
            em(lambda e, x1v=x1v, pb_=pb_: e.tensor_scalar(
                out=self.xh2_tm[pb_], in0=x1v, scalar1=1.0, scalar2=None,
                op0=ALU.mult, op1=ALU.add, accum_out=self.s_col[:]),
                waits=buf(f"xh2_tm{pb_}").wait_write() +
                buf("stat").wait_write())
            em(lambda e, x1v=x1v, pb_=pb_: e.scalar_tensor_tensor(
                out=self.xh2_tm[pb_], in0=x1v, scalar=1.0, in1=x1v,
                op0=ALU.mult, op1=ALU.mult, accum_out=self.q_col[:]))
            em(lambda e: e.tensor_scalar(
                out=self.m_col[:], in0=self.s_col[:], scalar1=1.0 / D,
                scalar2=None, op0=ALU.mult))
            em(lambda e: e.tensor_tensor(
                out=self.s_col[:], in0=self.m_col[:], in1=self.m_col[:],
                op=ALU.mult))
            t_var = em(lambda e: e.scalar_tensor_tensor(
                out=self.var_col[:], in0=self.q_col[:], scalar=1.0 / D,
                in1=self.s_col[:], op0=ALU.mult, op1=ALU.subtract))
            t_ln = sch.emit("act", lambda e: e.activation(
                self.rstd_col[:], self.var_col[:], AF.Ln,
                bias=self.eps_col[:], scale=1.0),
                waits=[t_var] + buf("eps").wait_read(), inc="sact")
            t_rstd = sch.emit("act", lambda e: e.activation(
                self.rstd_col[:], self.rstd_col[:], AF.Exp, bias=0.0,
                scale=-0.5), waits=[t_ln], inc="sact")
            buf("stat").set_written(t_rstd)
            t_n = sch.emit("dve", lambda e, x1v=x1v, pb_=pb_: e.tensor_scalar(
                out=self.xh2_tm[pb_], in0=x1v, scalar1=self.m_col[:],
                scalar2=self.rstd_col[:], op0=ALU.subtract, op1=ALU.mult),
                waits=[t_rstd, t_var], inc="sdve")
            buf(f"xh2_tm{pb_}").set_written(t_n)
            buf("stat").add_reader(t_n)
            buf("x1").add_reader(t_n)
            if pend[0] is not None:
                pend[0]()

            def mk_pend(o=o, pb_=pb_, t_n=t_n):
                def run():
                    for bi in range(4):
                        bank = bi % 2
                        pt = self.psb(bank, 1024)
                        bpb = buf(f"bank{bank}")
                        last = None
                        for k in range(8):
                            col = (bi * 8 + k) * P
                            last = sch.emit(
                                "pe", lambda e, col=col, k=k, pt=pt, pb_=pb_:
                                e.transpose(
                                    pt[:, k * P:(k + 1) * P],
                                    self.xh2_tm[pb_][:, col:col + P],
                                    self.ident[:]),
                                waits=([t_n] + bpb.wait_write())
                                if k == 0 else [],
                                inc="spe" if k == 7 else None)
                        bpb.set_written(last)
                        buf(f"xh2_tm{pb_}").add_reader(last)
                        dst = self.vb(self.O_R64, 32768).rearrange(
                            "p (fc j) -> p fc j", j=1024)[
                            :, bi * 8:(bi + 1) * 8, o * P:(o + 1) * P]
                        src2 = pt[:].rearrange("p (k j) -> p k j", j=P)
                        ev = sch.emit("act", lambda e, dst=dst, src2=src2:
                                      e.copy(dst, src2), waits=[last],
                                      inc="sact")
                        bpb.add_reader(ev)
                        buf("xh2").set_written(ev)
                    self.ffn_passA(o)
                return run
            pend[0] = mk_pend()
        pend[0]()

    def ffn_consts(self):
        sch, buf = self.sch, self.buf
        w0 = buf("xh1_own").wait_write()
        for name, dst, src in (("wr_sb", self.wr_sb, self.wr_d),
                               ("lA_sb", self.lA_sb, self.lA_d),
                               ("lB_sb", self.lB_sb, self.lB_d),
                               ("bb_sb", self.bb_sb, self.bb_d),
                               ("br_sb", self.br_sb[:], self.br_d)):
            tok = self.pool_dma(dst, src, waits=w0)
            buf(name).set_written(tok)
        t_on = sch.emit("pool", lambda e: e.memset(self.ones_col[:], 1.0),
                        inc="spool")
        buf("ones_col").set_written(t_on)
        t_io = sch.emit("pool", lambda e: e.iota(
            self.kiota[:], pattern=[[1, 8]], base=1, channel_multiplier=0,
            allow_small_or_imprecise_dtypes=True),
            inc="spool")
        buf("kiota").set_written(t_io)

    def ffn_passA(self, o):
        sch, buf = self.sch, self.buf
        if True:
            rb_, hb_ = (2, 3) if o % 2 == 0 else (6, 7)
            b0, b1 = buf(f"bank{rb_}"), buf(f"bank{hb_}")
            lasth = None
            for fc in range(NFC):
                w = []
                if fc == 0:
                    w = (b0.wait_write() + b1.wait_write() +
                         buf("xh2").wait_read() + buf("wr_sb").wait_read() +
                         buf("lA_sb").wait_read())
                lhs = self.vb(self.O_R64 + fc * 2048 + o * 256, P)
                sch.emit("pe", lambda e, fc=fc, lhs=lhs, rb_=rb_: e.matmul(
                    self.psf(rb_, 8), lhs, self.wr_sb[:, fc * 8:(fc + 1) * 8],
                    start=(fc == 0), stop=False, skip_group_check=True),
                    waits=w)
                lasth = sch.emit("pe", lambda e, fc=fc, lhs=lhs, hb_=hb_:
                                 e.matmul(
                    self.psf(hb_, P), lhs, self.lA_sb[:, fc * P:(fc + 1) * P],
                    start=(fc == 0), stop=(fc == NFC - 1),
                    skip_group_check=True),
                    inc="spe" if fc == NFC - 1 else None)
            t_rb = sch.emit("pe", lambda e, rb_=rb_: e.matmul(
                self.psf(rb_, 8), self.ones_col[:], self.br_sb[:],
                start=False, stop=True, skip_group_check=True),
                waits=buf("br_sb").wait_read() + buf("ones_col").wait_read(),
                inc="spe")
            b0.set_written(t_rb)
            b1.set_written(lasth)
            em = sch.chain("dve")
            t_z = em(lambda e, rb_=rb_: e.tensor_copy(
                self.zbuf[:], self.psf(rb_, 8)),
                waits=[t_rb] + buf("spx").wait_write())
            b0.add_reader(t_z)
            em(lambda e: e.tensor_copy(self.zs0[:], self.zbuf[:]))
            cur, nxt = self.zs0, self.zs1
            stages = [
                [(0, 1), (2, 3), (4, 5), (6, 7)],
                [(0, 2), (1, 3), (4, 6), (5, 7)],
                [(1, 2), (5, 6)],
                [(0, 4), (1, 5), (2, 6), (3, 7)],
                [(2, 4), (3, 5)],
                [(1, 2), (3, 4), (5, 6)],
            ]
            for st in stages:
                covered = set()
                for (a, b) in st:
                    covered.add(a)
                    covered.add(b)
                    em(lambda e, a=a, b=b, cur=cur, nxt=nxt:
                       e.tensor_tensor(nxt[:, a:a + 1], cur[:, a:a + 1],
                                       cur[:, b:b + 1], op=ALU.max))
                    em(lambda e, a=a, b=b, cur=cur, nxt=nxt:
                       e.tensor_tensor(nxt[:, b:b + 1], cur[:, a:a + 1],
                                       cur[:, b:b + 1], op=ALU.min))
                for j in range(8):
                    if j not in covered:
                        em(lambda e, j=j, cur=cur, nxt=nxt:
                           e.tensor_copy(nxt[:, j:j + 1], cur[:, j:j + 1]))
                cur, nxt = nxt, cur
            zs = cur
            em(lambda e, zs=zs: e.tensor_copy(
                self.cums[:, 0:1], zs[:, 0:1]))
            em(lambda e, zs=zs: e.tensor_tensor(
                self.cums[:, 1:8], zs[:, 1:8], zs[:, 0:7], op=ALU.add))
            em(lambda e, nxt=nxt: e.tensor_copy(
                nxt[:, 0:2], self.cums[:, 0:2]))
            em(lambda e, nxt=nxt: e.tensor_tensor(
                nxt[:, 2:8], self.cums[:, 2:8], self.cums[:, 0:6], op=ALU.add))
            em(lambda e, nxt=nxt: e.tensor_copy(
                self.cums[:, 0:4], nxt[:, 0:4]))
            em(lambda e, nxt=nxt: e.tensor_tensor(
                self.cums[:, 4:8], nxt[:, 4:8], nxt[:, 0:4], op=ALU.add))
            em(lambda e, zs=zs, nxt=nxt: e.tensor_tensor(
                nxt[:], zs[:], self.kiota[:], op=ALU.mult),
                waits=buf("kiota").wait_read())
            em(lambda e: e.tensor_scalar(
                out=self.cums[:], in0=self.cums[:], scalar1=-1.0,
                scalar2=None, op0=ALU.add))
            em(lambda e, nxt=nxt: e.tensor_tensor(
                nxt[:], nxt[:], self.cums[:], op=ALU.is_gt))
            em(lambda e, nxt=nxt: e.reduce_sum(
                self.kz[:, 0:1], nxt[:], axis=AX.X))
            em(lambda e, zs=zs, nxt=nxt: e.scalar_tensor_tensor(
                out=zs[:], in0=nxt[:], scalar=1.0, in1=zs[:],
                op0=ALU.mult, op1=ALU.mult, accum_out=self.kz[:, 1:2]))
            em(lambda e: e.reciprocal(
                self.kz[:, 2:3], self.kz[:, 0:1]))
            em(lambda e: e.tensor_scalar(
                out=self.kz[:, 1:2], in0=self.kz[:, 1:2], scalar1=-1.0,
                scalar2=None, op0=ALU.add))
            em(lambda e: e.tensor_tensor(
                self.kz[:, 3:4], self.kz[:, 1:2], self.kz[:, 2:3],
                op=ALU.mult))
            t_w = em(lambda e: e.tensor_scalar(
                out=self.zbuf[:], in0=self.zbuf[:], scalar1=self.kz[:, 3:4],
                scalar2=0.0, op0=ALU.subtract, op1=ALU.max))
            buf("spx").set_written(t_w)
            bhw = buf("hw")
            t_hw = sch.emit("dve", lambda e, hb_=hb_: e.tensor_tensor(
                self.hw[:].rearrange("p (e r) -> p e r", r=16),
                self.psf(hb_, P).rearrange("p (e r) -> p e r", r=16),
                self.zbuf[:, :, None].broadcast_to([P, 8, 16]),
                op=ALU.mult),
                waits=[t_w, lasth] + bhw.wait_write(), inc="sdve")
            b1.add_reader(t_hw)
            bhw.set_written(t_hw)
            bpb = buf("bank5")
            pt = self.psb(5, 1024)
            t_tr = sch.emit("pe", lambda e, pt=pt: e.transpose(
                pt[:, 0:P], self.hw[:], self.ident[:]),
                waits=[t_hw] + bpb.wait_write(), inc="spe")
            bpb.set_written(t_tr)
            bhw.add_reader(t_tr)
            t_ht = sch.emit("dve", lambda e, o=o, pt=pt: e.tensor_copy(
                self.hwT[:, o * P:(o + 1) * P], pt[:, 0:P]),
                waits=[t_tr] + buf("h_wT").wait_write(), inc="sdve")
            bpb.add_reader(t_ht)
            buf("h_wT").set_written(t_ht)

    def ffn(self):
        sch, buf = self.sch, self.buf
        for og in range(8):
            qwbs = []
            for q in range(2):
                wb = buf(f"dq{q}")
                tok = self.pool_dma(
                    self.dq[q],
                    self.wb_d[og][:, q * 8192:(q + 1) * 8192],
                    waits=wb.wait_write())
                wb.set_written(tok)
                qwbs.append(wb)
            pbs = [buf(f"bank{b}") for b in range(8)]
            last_by_o = [None] * 8
            for fc in range(NFC):
                q = fc // 16
                wb = qwbs[q]
                wcol = (fc % 16) * 512
                for o in range(8):
                    w = []
                    if fc == 0:
                        w += pbs[o].wait_write() + buf("xh2").wait_read()
                    if fc % 16 == 0:
                        w += wb.wait_read()
                    lhs = self.vb(self.O_R64 + fc * 2048 + o * 256, P)
                    last_by_o[o] = sch.emit(
                        "pe", lambda e, fc=fc, o=o, q=q, wcol=wcol, lhs=lhs:
                        e.matmul(self.psf(o, 512), lhs,
                                 self.dq[q][:, wcol:wcol + 512],
                                 start=(fc == 0), stop=False,
                                 skip_group_check=True),
                        waits=w,
                        inc="spe" if (fc == NFC - 1 or
                                      (fc % 16 == 15 and o == 7)) else None)
                if fc % 16 == 15:
                    wb.add_reader(last_by_o[7])
            for o in range(8):
                sch.emit("pe", lambda e, o=o, og=og: e.matmul(
                    self.psf(o, 512),
                    self.hwT[:, o * P:(o + 1) * P],
                    self.lB_sb[:, og * 512:(og + 1) * 512],
                    start=False, stop=False, skip_group_check=True),
                    waits=buf("h_wT").wait_read() + buf("lB_sb").wait_read())
                last = sch.emit("pe", lambda e, o=o, og=og: e.matmul(
                    self.psf(o, 512), self.ones_col[:],
                    self.bb_sb[0:1, og * 512:(og + 1) * 512],
                    start=False, stop=True, skip_group_check=True),
                    waits=buf("bb_sb").wait_read(), inc="spe")
                x1v = self.vb(self.O_KV + o * 8192 + og * 1024, 512)
                last = sch.emit("pe", lambda e, o=o, x1v=x1v: e.matmul(
                    self.psf(o, 512), self.ident[:], x1v,
                    start=False, stop=True, skip_group_check=True),
                    waits=[last] + buf("x1").wait_read(), inc="spe")
                pbs[o].set_written(last)
                bo = buf(f"op{o % 2}")
                t_ev = sch.emit("dve", lambda e, o=o:
                                e.tensor_copy(self.op_[o % 2],
                                              self.psf(o, 512)),
                                waits=[last] + bo.wait_write(), inc="sdve")
                pbs[o].add_reader(t_ev)
                bo.set_written(t_ev)
                t_st = sch.emit("sync", lambda e, o=o, og=og: e.dma_start(
                    out=self.y_d[o * P:(o + 1) * P, og * 512:(og + 1) * 512],
                    in_=self.op_[o % 2]),
                    waits=[t_ev], inc="sst", amt=16)
                bo.add_reader(t_st)


def build_masks(p):
    m = np.zeros((NMASK, P, P), dtype=np.float32)
    tri = np.triu(np.full((P, P), NEG, dtype=np.float32), k=1)
    for cls, mi in _PAIRS.items():
        c = cls[p]
        if c == "tri":
            m[mi] = tri
        elif c == "neg":
            m[mi] = NEG
    return np.ascontiguousarray(
        m.transpose(1, 0, 2).reshape(P, NMASK * P)).astype(ml_dtypes.bfloat16)


def _pack_weights(inputs):
    d = {}
    d["wqkv_p"] = np.ascontiguousarray(
        np.asarray(inputs["L0_attn_w_qkv"], dtype=np.float32)
        .reshape(48, 128, 32, 128).transpose(0, 3, 2, 1).reshape(48, 128, 4096))
    d["wd_p"] = np.ascontiguousarray(
        np.asarray(inputs["L0_attn_w_dense"], dtype=np.float32)
        .reshape(8, 512, 32, 128).transpose(0, 3, 2, 1).reshape(8, 128, 16384))
    d["wb_p"] = np.ascontiguousarray(
        np.asarray(inputs["L0_ffn_W_base"], dtype=np.float32)
        .reshape(8, 512, 32, 128).transpose(0, 3, 2, 1).reshape(8, 128, 16384))
    d["wr_p"] = np.ascontiguousarray(
        np.asarray(inputs["L0_ffn_W_router"], dtype=np.float32)
        .T.reshape(32, 128, 8).transpose(1, 0, 2).reshape(128, 256))
    d["lA_p"] = np.ascontiguousarray(
        np.asarray(inputs["L0_ffn_lora_A"], dtype=np.float32)
        .transpose(1, 0, 2).reshape(32, 128, 128).transpose(1, 0, 2)
        .reshape(128, 4096))
    d["lB_p"] = np.ascontiguousarray(
        np.asarray(inputs["L0_ffn_lora_B"], dtype=np.float32).reshape(128, 4096))
    d["bb"] = np.ascontiguousarray(
        np.asarray(inputs["L0_ffn_b_base"], dtype=np.float32).reshape(1, D))
    d["br"] = np.ascontiguousarray(
        np.asarray(inputs["L0_ffn_b_router"], dtype=np.float32).reshape(1, 8))
    return d


def kernel(**inputs):
    x = np.asarray(inputs["L0_attn_q"], dtype=np.float32)
    shared = _pack_weights(inputs)
    kb = KB()
    nc = kb.build()
    in_maps = []
    for c in range(8):
        b, p = c // 2, c % 2
        perm = np.concatenate(
            [np.arange(t * P, (t + 1) * P) for t in col_order(p)])
        m = dict(shared)
        m["x_perm"] = np.ascontiguousarray(x[b][perm])
        m["masks"] = build_masks(p)
        in_maps.append(m)
    res = run_bass_kernel_spmd(nc, in_maps, core_ids=list(range(8)))
    y = np.empty((B, S, D), dtype=np.float32)
    for c in range(8):
        b, p = c // 2, c % 2
        out = res.results[c]["y_out"]
        for o, t in enumerate(T_SETS[p]):
            y[b, t * P:(t + 1) * P, :] = out[o * P:(o + 1) * P, :]
    return y


# version 19
# speedup vs baseline: 1.0018x; 1.0018x over previous
import math
import numpy as np
import ml_dtypes
import concourse.bass as bass
import concourse.mybir as mybir
from concourse.bass_utils import run_bass_kernel_spmd

F32 = mybir.dt.float32
BF16 = mybir.dt.bfloat16
AF = mybir.ActivationFunctionType
ALU = mybir.AluOpType
AX = mybir.AxisListType

B, S, D = 4, 2048, 4096
P = 128
NFC = 32
NKVH = 8
SCALE = 1.0 / math.sqrt(128.0)
EPS = 1e-5
NEG = -30000.0

T_SETS = ([0, 3, 4, 7, 8, 11, 12, 15], [1, 2, 5, 6, 9, 10, 13, 14])
OWN_ROW = [0, 1, 2, 3, 8, 9, 10, 11]


def col_order(p):
    own, oth = T_SETS[p], T_SETS[1 - p]
    return own[0:4] + oth[0:4] + own[4:8] + oth[4:8]


def attn_structure():
    cols = {p: col_order(p) for p in (0, 1)}
    slots = []
    mask_index = {}
    mi = 0
    for s in range(8):
        cq = s if s < 4 else s + 4
        blocks = []
        for sec in range(4):
            base = sec * 4
            pref = 0
            for p in (0, 1):
                gq = cols[p][cq]
                n = sum(1 for k in range(4) if cols[p][base + k] <= gq)
                pref = max(pref, n)
            blocks += [base + k for k in range(pref)]
        masked = [c for c in blocks
                  if any(cols[p][c] >= cols[p][cq] for p in (0, 1))]
        for c in masked:
            mask_index[(s, c)] = mi
            mi += 1
        slots.append((blocks, masked))
    return cols, slots, mask_index, mi


COLS, SLOTS, _MASK_INDEX_FULL, _NMASK_FULL = attn_structure()
_PAIRS = {}
MASK_INDEX = {}
for _s in range(8):
    _cq = _s if _s < 4 else _s + 4
    for _c in SLOTS[_s][1]:
        _cls = tuple(
            ("tri" if COLS[_p][_c] == COLS[_p][_cq] else
             ("neg" if COLS[_p][_c] > COLS[_p][_cq] else "zero"))
            for _p in (0, 1))
        if _cls not in _PAIRS:
            _PAIRS[_cls] = len(_PAIRS)
        MASK_INDEX[(_s, _c)] = _PAIRS[_cls]
NMASK = len(_PAIRS)

ENGS = ("pe", "dve", "act", "pool", "sync")
SEM_ENG = {"spe": "pe", "sdve": "dve", "sact": "act", "spool": "pool"}


class Sched:
    def __init__(self):
        self.streams = {e: [] for e in ENGS}
        self.cnt = {}
        self.waited = {e: {} for e in ENGS}
        self.sems = {}
        self.guards = []

    def new_sem(self, nc, name):
        g = nc.semaphore(name)
        self.sems[name] = g.__enter__()
        self.guards.append(g)
        self.cnt[name] = 0

    def wait(self, eng, tok):
        if tok is None:
            return
        name, val = tok
        w = self.waited[eng]
        if w.get(name, 0) >= val:
            return
        w[name] = val
        h = self.sems[name]
        self.streams[eng].append(lambda e, h=h, v=val: e.wait_ge(h, v))

    def emit(self, eng, fn, waits=(), inc=None, amt=1):
        for t in waits:
            self.wait(eng, t)
        if inc is None:
            self.streams[eng].append(fn)
            return None
        self.cnt[inc] += amt
        tok = (inc, self.cnt[inc])
        h = self.sems[inc]
        self.streams[eng].append(
            lambda e, fn=fn, h=h, a=amt: fn(e).then_inc(h, a))
        return tok

    def chain(self, eng):
        sem = {"dve": "sdve", "act": "sact", "pool": "spool",
               "pe": "spe"}[eng]
        state = {"tok": None}

        def em(fn, waits=()):
            w = list(waits)
            if state["tok"] is not None:
                w.append(state["tok"])
            t = self.emit(eng, fn, waits=w, inc=sem)
            state["tok"] = t
            return t
        return em

    def fence(self):
        toks = []
        fs = self.fence_scratch
        ident = self.fence_ident
        ops = {"pe": lambda en: en.ldweights(ident[:]),
               "dve": lambda en: en.memset(fs[0:1, 0:1], 0.0),
               "act": lambda en: en.activation(fs[0:1, 0:1], fs[0:1, 1:2],
                                               AF.Identity, bias=0.0,
                                               scale=0.0),
               "pool": lambda en: en.memset(fs[0:1, 1:2], 0.0)}
        for e in ("pe", "dve", "act", "pool"):
            sem = "f_" + e
            self.cnt[sem] += 1
            toks.append((sem, self.cnt[sem]))
            h = self.sems[sem]
            op = ops[e]
            self.streams[e].append(lambda en, h=h, op=op: op(en).then_inc(h, 1))
        toks.append(("sx", self.cnt["sx"]))
        toks.append(("sw", self.cnt["sw"]))
        for e in ENGS:
            for t in toks:
                self.wait(e, t)


class Buf:
    def __init__(self):
        self.w = {}
        self.r = {}

    def wait_write(self):
        return list(self.w.items()) + list(self.r.items())

    def wait_read(self):
        return list(self.w.items())

    def set_written(self, tok):
        if tok is None:
            return
        s, v = tok
        self.w[s] = max(self.w.get(s, 0), v)

    def add_reader(self, tok):
        if tok is None:
            return
        s, v = tok
        self.r[s] = max(self.r.get(s, 0), v)


class KB:
    def __init__(self, stop_after=None, taps=(), attn_units=None,
                 attn_stage=99):
        self.stop_after = stop_after
        self.tap_list = list(taps)
        self.attn_units = attn_units
        self.attn_stage = attn_stage
        self._unit_count = 0
        self._pend = None
        self.nc = bass.Bass("TRN2", target_bir_lowering=False,
                            num_swdge_queues=4,
                            detect_race_conditions=False)
        self.sch = Sched()
        self.bufs = {}
        self.guards = []
        self.pool_dma_toks = []

    def buf(self, name):
        if name not in self.bufs:
            self.bufs[name] = Buf()
        return self.bufs[name]

    def sb(self, name, shape, dt):
        g = self.nc.sbuf_tensor(name, shape, dt)
        t = g.__enter__()
        self.guards.append(g)
        return t

    def pool_dma(self, dst, src, waits=()):
        w = list(waits)
        if len(self.pool_dma_toks) >= 4:
            w.append(self.pool_dma_toks[-4])
        tok = self.sch.emit("pool", lambda e, d=dst, s=src:
                            e.dma_start(out=d, in_=s),
                            waits=w, inc="sw", amt=16)
        self.pool_dma_toks.append(tok)
        return tok

    def build(self):
        nc, sch = self.nc, self.sch
        for s in ("sx", "sw", "spe", "sact", "sdve", "spool", "sst",
                  "f_pe", "f_dve", "f_act", "f_pool"):
            sch.new_sem(nc, s)

        self.x_d = nc.dram_tensor("x_perm", [S, D], F32, kind="ExternalInput").ap()
        self.mask_d = nc.dram_tensor("masks", [P, NMASK * P], BF16,
                                     kind="ExternalInput").ap()
        self.wqkv_d = nc.dram_tensor("wqkv_p", [48, P, 4096], F32,
                                     kind="ExternalInput").ap()
        self.wd_d = nc.dram_tensor("wd_p", [8, P, 16384], F32,
                                   kind="ExternalInput").ap()
        self.wb_d = nc.dram_tensor("wb_p", [8, P, 16384], F32,
                                   kind="ExternalInput").ap()
        self.wr_d = nc.dram_tensor("wr_p", [P, 256], F32,
                                   kind="ExternalInput").ap()
        self.lA_d = nc.dram_tensor("lA_p", [P, 4096], F32,
                                   kind="ExternalInput").ap()
        self.lB_d = nc.dram_tensor("lB_p", [P, 4096], F32,
                                   kind="ExternalInput").ap()
        self.bb_d = nc.dram_tensor("bb", [1, D], F32, kind="ExternalInput").ap()
        self.br_d = nc.dram_tensor("br", [1, 8], F32, kind="ExternalInput").ap()
        self.y_d = nc.dram_tensor("y_out", [1024, D], F32,
                                  kind="ExternalOutput").ap()

        AR_F32 = 52480
        g = nc.sbuf_tensor("arena", [P, AR_F32], F32)
        self.arena = g.__enter__()
        self.guards.append(g)
        self.ar_h = self.arena.ap().tensor
        self.ar_bf = self.ar_h.bitcast(BF16)

        self.O_REGA = 0
        self.O_R64 = 32768
        self.O_KV = 98304
        self.O_D = 163840
        self.O_E = 176128

        self.xp = [self.vf(self.O_E, 512), self.vf(self.O_E + 2048, 512)]
        self.xh1_tm = self.vb(self.O_E + 4096, 2048)
        self.v_tmp = self.vb(self.O_E + 8192, 512)
        self.probs = self.vb(self.O_E + 9216, 2048)
        self.probs_n = [self.vb(self.O_E + 13312, 1024),
                        self.vb(self.O_E + 15360, 1024)]
        self.probsT_all = self.vb(self.O_E + 17408, 8192)
        self.wbuf = [self.vb(self.O_D + i * 4096, 2048) for i in range(3)]
        self.dq = [self.vb(self.O_E + i * 16384, 8192) for i in range(2)]
        self.xs2 = [self.vb(self.O_D, 2048), self.vb(self.O_D + 4096, 2048)]
        self.xh2_tm = [self.vb(self.O_E, 4096), self.vb(self.O_E + 8192, 4096)]
        self.lA_sb = self.vb(self.O_REGA, 4096)
        self.lB_sb = self.vb(self.O_REGA + 8192, 4096)
        self.bb_sb = self.vb(self.O_REGA + 16384, 4096, rows=1)
        self.wr_sb = self.vb(self.O_REGA + 24576, 256)
        self.hwT = self.vb(self.O_REGA + 25600, 1024)
        self.op_ = [self.vf(self.O_D, 512), self.vf(self.O_D + 2048, 512)]

        sb = self.sb
        self.ident = sb("ident", [P, P], BF16)
        self.masks_sb = sb("masks_sb", [P, NMASK * P], BF16)
        self.sum_part = sb("sum_part", [P, 8], F32)
        self.sumsq_part = sb("sumsq_part", [P, 8], F32)
        self.s_col = sb("s_col", [P, 1], F32)
        self.q_col = sb("q_col", [P, 1], F32)
        self.m_col = sb("m_col", [P, 1], F32)
        self.var_col = sb("var_col", [P, 1], F32)
        self.rstd_col = sb("rstd_col", [P, 1], F32)
        self.nsm = sb("nsm", [P, 2], F32)
        self.l_part = sb("l_part", [P, 2], F32)
        self.l_col = sb("l_col", [P, 2], F32)
        self.r_col = sb("r_col", [P, 2], F32)
        self.zbuf = sb("zbuf", [P, 8], F32)
        self.zs0 = sb("zs0", [P, 8], F32)
        self.zs1 = sb("zs1", [P, 8], F32)
        self.cums = sb("cums", [P, 8], F32)
        self.kz = sb("kz", [P, 4], F32)
        self.kiota = sb("kiota", [P, 8], F32)
        self.ones_col = sb("ones_col", [1, P], BF16)
        self.br_sb = sb("br_sb", [1, 8], BF16)
        self.hw = sb("hw", [P, P], BF16)
        self.eps_col = sb("eps_col", [P, 1], F32)
        self.fence_scratch = sb("fence_scratch", [1, 2], F32)
        sch.fence_scratch = self.fence_scratch
        sch.fence_ident = self.ident

        g = nc.psum_tensor("psA", [P, 4096], F32)
        self.psum = g.__enter__()
        self.guards.append(g)
        self.ps_h = self.psum.ap().tensor
        self.ps_bf = self.ps_h.bitcast(BF16)

        def attn0_bg():
            bg = self.p1_gen(1)
            self.attn(0, bg=bg)
            for _ in bg:
                pass
        phases = [("p0", lambda: self.p0()),
                  ("p1g0", lambda: self.p1(0)),
                  ("p2g0", lambda: self.p2(0)),
                  ("attn0", attn0_bg),
                  ("p1g1", lambda: None),
                  ("p2g1", lambda: self.p2(1)),
                  ("consts", lambda: self.ffn_consts()),
                  ("attn1", lambda: self.attn(1)),
                  ("fence1", lambda: sch.fence()),
                  ("dense", lambda: self.dense()),
                  ("fence2", lambda: sch.fence()),
                  ("ffn", lambda: self.ffn())]
        for name, fn in phases:
            fn()
            if self.stop_after == name:
                break
        if self.tap_list:
            sch.fence()
            for name, off, cols in self.tap_list:
                tap_d = nc.dram_tensor(name, [P, cols], F32,
                                       kind="ExternalOutput").ap()
                if isinstance(off, str):
                    src_ap = getattr(self, off)[:] if cols is None else \
                        getattr(self, off)[:, 0:cols]
                else:
                    src_ap = self.vf(off, cols)
                sch.emit("sync", lambda e, d=tap_d, s=src_ap:
                         e.dma_start(out=d, in_=s), inc="sst", amt=16)
        sch.wait("sync", ("sst", sch.cnt["sst"]))

        with nc.Block() as block:
            block.tensor(lambda e: [f(e) for f in sch.streams["pe"]])
            block.vector(lambda e: [f(e) for f in sch.streams["dve"]])
            block.scalar(lambda e: [f(e) for f in sch.streams["act"]])
            block.gpsimd(lambda e: [f(e) for f in sch.streams["pool"]])
            block.sync(lambda e: [f(e) for f in sch.streams["sync"]])

        for g in reversed(self.guards):
            g.__exit__(None, None, None)
        return nc

    def vf(self, off, cols, rows=P):
        return self.ar_h[0:rows, off // 4: off // 4 + cols]

    def vb(self, off, cols, rows=P):
        return self.ar_bf[0:rows, off // 2: off // 2 + cols]

    def psf(self, bank, cols, off=0):
        c0 = bank * 512 + off
        return self.ps_h[0:P, c0:c0 + cols]

    def psb(self, bank, cols, off=0):
        c0 = bank * 1024 + off
        return self.ps_bf[0:P, c0:c0 + cols]

    def p0(self):
        sch, buf = self.sch, self.buf
        t0 = sch.emit("pool", lambda e: e.memset(self.ident[:], 0.0),
                      inc="spool")
        t = sch.emit("pool", lambda e: e.affine_select(
            out=self.ident[:], in_=self.ident[:], pattern=[[-1, P]],
            compare_op=ALU.not_equal, fill=1.0, base=0, channel_multiplier=1),
            waits=[t0], inc="spool")
        buf("ident").set_written(t)
        t = sch.emit("sync", lambda e: e.dma_start(
            out=self.masks_sb[:], in_=self.mask_d), inc="sx", amt=16)
        buf("masks").set_written(t)
        t = sch.emit("pool", lambda e: e.memset(self.eps_col[:], EPS),
                     inc="spool")
        buf("eps").set_written(t)

    def p1(self, g):
        for _ in self.p1_gen(g):
            pass

    def p1_gen(self, g):
        sch, buf = self.sch, self.buf
        for c in range(8):
            row0 = (g * 8 + c) * P
            t2 = None
            for pc in range(8):
                bx = buf(f"xp{pc % 2}")
                tok = sch.emit("sync", lambda e, pc=pc, row0=row0: e.dma_start(
                    out=self.xp[pc % 2],
                    in_=self.x_d[row0:row0 + P, pc * 512:(pc + 1) * 512]),
                    waits=bx.wait_write(), inc="sx", amt=16)
                bx.set_written(tok)
                w1 = bx.wait_read()
                if pc == 0:
                    w1 = w1 + buf("xh1_tm").wait_write()
                sch.emit("dve", lambda e, pc=pc: e.tensor_scalar(
                    out=self.xh1_tm[:, 0:512], in0=self.xp[pc % 2],
                    scalar1=1.0, scalar2=None, op0=ALU.mult, op1=ALU.add,
                    accum_out=self.sum_part[:, pc:pc + 1]), waits=w1)
                t2 = sch.emit("dve", lambda e, pc=pc: e.scalar_tensor_tensor(
                    out=self.xh1_tm[:, 512:1024], in0=self.xp[pc % 2],
                    scalar=1.0, in1=self.xp[pc % 2],
                    op0=ALU.mult, op1=ALU.mult,
                    accum_out=self.sumsq_part[:, pc:pc + 1]),
                    inc="sdve")
                bx.add_reader(t2)
            em = sch.chain("dve")
            em(lambda e: e.reduce_sum(
                self.s_col[:], self.sum_part[:], axis=AX.X),
                waits=buf("stat").wait_write() + [t2])
            em(lambda e: e.reduce_sum(
                self.q_col[:], self.sumsq_part[:], axis=AX.X))
            em(lambda e: e.tensor_scalar(
                out=self.m_col[:], in0=self.s_col[:], scalar1=1.0 / D,
                scalar2=None, op0=ALU.mult))
            em(lambda e: e.tensor_tensor(
                out=self.s_col[:], in0=self.m_col[:], in1=self.m_col[:],
                op=ALU.mult))
            t_var = em(lambda e: e.scalar_tensor_tensor(
                out=self.var_col[:], in0=self.q_col[:], scalar=1.0 / D,
                in1=self.s_col[:], op0=ALU.mult, op1=ALU.subtract))
            t_ln = sch.emit("act", lambda e: e.activation(
                self.rstd_col[:], self.var_col[:], AF.Ln,
                bias=self.eps_col[:], scale=1.0),
                waits=[t_var] + buf("eps").wait_read(), inc="sact")
            t_rstd = sch.emit("act", lambda e: e.activation(
                self.rstd_col[:], self.rstd_col[:], AF.Exp, bias=0.0,
                scale=-0.5), waits=[t_ln], inc="sact")
            buf("stat").set_written(t_rstd)
            for half in range(2):
                t_norm = None
                for k4 in range(4):
                    pc = half * 4 + k4
                    bx = buf(f"xp{pc % 2}")
                    tok = sch.emit("sync", lambda e, pc=pc, row0=row0:
                                   e.dma_start(
                                       out=self.xp[pc % 2],
                                       in_=self.x_d[row0:row0 + P,
                                                    pc * 512:(pc + 1) * 512]),
                                   waits=bx.wait_write(), inc="sx", amt=16)
                    bx.set_written(tok)
                    w1 = bx.wait_read() + [t_rstd, t_var]
                    if k4 == 0:
                        w1 = w1 + buf("xh1_tm").wait_write()
                    t_norm = sch.emit("dve", lambda e, pc=pc, k4=k4:
                                      e.tensor_scalar(
                                          out=self.xh1_tm[:, k4 * 512:
                                                          (k4 + 1) * 512],
                                          in0=self.xp[pc % 2],
                                          scalar1=self.m_col[:],
                                          scalar2=self.rstd_col[:],
                                          op0=ALU.subtract, op1=ALU.mult),
                                      waits=w1,
                                      inc="sdve")
                    bx.add_reader(t_norm)
                buf("xh1_tm").set_written(t_norm)
                buf("stat").add_reader(t_norm)
                for bi in range(2):
                    bank = 4 + bi
                    pt = self.psb(bank, 1024)
                    bpb = buf(f"bank{bank}")
                    last = None
                    for k in range(8):
                        col = (bi * 8 + k) * P
                        last = sch.emit("pe", lambda e, col=col, k=k, pt=pt:
                                        e.transpose(
                                            pt[:, k * P:(k + 1) * P],
                                            self.xh1_tm[:, col:col + P],
                                            self.ident[:]),
                                        waits=([t_norm] + bpb.wait_write() +
                                               buf("ident").wait_read())
                                        if k == 0 else [],
                                        inc="spe" if k == 7 else None)
                    bpb.set_written(last)
                    buf("xh1_tm").add_reader(last)
                    fc0 = half * 16 + bi * 8
                    if c < 4:
                        dst = self.vb(self.O_REGA, 16384).rearrange(
                            "p (fc j) -> p fc j", j=512)[
                            :, fc0:fc0 + 8, c * P:(c + 1) * P]
                        bname = "xh1_own"
                    else:
                        dst = self.vb(self.O_R64, 32768).rearrange(
                            "p (fc j) -> p fc j", j=1024)[
                            :, fc0:fc0 + 8,
                            512 + (c - 4) * P: 512 + (c - 3) * P]
                        bname = "xh1_oth"
                    src = pt[:].rearrange("p (k j) -> p k j", j=P)
                    if bi == 0:
                        ev = sch.emit("act", lambda e, dst=dst, src=src:
                                      e.copy(dst, src),
                                      waits=[last] + buf(bname).wait_write(),
                                      inc="sact")
                    else:
                        ev = sch.emit("dve", lambda e, dst=dst, src=src:
                                      e.tensor_copy(dst, src),
                                      waits=[last] + buf(bname).wait_write(),
                                      inc="sdve")
                    bpb.add_reader(ev)
                    buf(bname).set_written(ev)
            yield c

    def p2(self, g):
        sch, buf = self.sch, self.buf
        oc_order = list(range(32, 40)) + list(range(40, 48)) + list(range(32))
        si = 0
        for oc in oc_order:
            is_q = oc < 32
            halves = []
            for h in range(2):
                wb = buf(f"wbuf{si % 3}")
                tok = self.pool_dma(
                    self.wbuf[si % 3],
                    self.wqkv_d[oc][:, h * 2048:(h + 1) * 2048],
                    waits=wb.wait_write())
                wb.set_written(tok)
                halves.append((self.wbuf[si % 3], wb))
                si += 1
            if is_q:
                banks = [6 + (oc % 2)]
                npiece = 1
            else:
                banks = [2 + 2 * (oc % 2), 3 + 2 * (oc % 2)]
                npiece = 2
            pb = [buf(f"bank{b}") for b in banks]
            last = None
            for fc in range(NFC):
                wsb, wbb = halves[fc // 16]
                wcol = (fc % 16) * P
                for piece in range(npiece):
                    if is_q or piece == 0:
                        rhs = self.vb(self.O_REGA, 16384).rearrange(
                            "p (fc j) -> p fc j", j=512)[:, fc, :]
                    else:
                        rhs = self.vb(self.O_R64, 32768).rearrange(
                            "p (fc j) -> p fc j", j=1024)[:, fc, 512:1024]
                    bank = banks[piece]
                    w = []
                    if fc == 0 and piece == 0:
                        for b_ in pb:
                            w += b_.wait_write()
                        w += buf("xh1_own").wait_read()
                        if not is_q:
                            w += buf("xh1_oth").wait_read()
                    if fc % 16 == 0 and piece == 0:
                        w = w + wbb.wait_read()
                    last = sch.emit("pe", lambda e, wsb=wsb, wcol=wcol,
                                    bank=bank, rhs=rhs, fc=fc:
                                    e.matmul(self.psf(bank, 512),
                                             wsb[:, wcol:wcol + P], rhs,
                                             start=(fc == 0),
                                             stop=(fc == NFC - 1),
                                             skip_group_check=True),
                                    waits=w,
                                    inc="spe" if (fc == NFC - 1 and
                                                  piece == npiece - 1)
                                    else None)
            for h_ in halves:
                h_[1].add_reader(last)
            buf("xh1_own").add_reader(last)
            if not is_q:
                buf("xh1_oth").add_reader(last)
            for b_ in pb:
                b_.set_written(last)
            if is_q:
                dst = self.vb(self.O_R64 + oc * 2048 + g * 1024, 512)
                ev = sch.emit("act", lambda e, dst=dst, b0=banks[0]:
                              e.activation(dst, self.psf(b0, 512), AF.Copy,
                                           bias=0.0, scale=SCALE),
                              waits=[last], inc="sact")
                pb[0].add_reader(ev)
                buf(f"q{g}").set_written(ev)
            elif oc < 40:
                kvh = oc - 32
                dst = self.vb(self.O_KV + kvh * 4096 + g * 2048, 1024)
                ev = sch.emit("act", lambda e, dst=dst, b0=banks[0]:
                              e.copy(dst, self.psf(b0, 1024)),
                              waits=[last], inc="sact")
                pb[0].add_reader(ev)
                pb[1].add_reader(ev)
                buf(f"k{g}").set_written(ev)
            else:
                kvh = oc - 40
                for hv in range(2):
                    bv = buf("v_tmp")
                    ev = sch.emit("act", lambda e, b0=banks[0], hv=hv:
                                  e.copy(self.v_tmp,
                                         self.psf(b0, 512, off=hv * 512)),
                                  waits=[last] + bv.wait_write(), inc="sact")
                    bv.set_written(ev)
                    pb[0].add_reader(ev)
                    pb[1].add_reader(ev)
                    bank = oc % 2
                    pt = self.psb(bank, 512, off=hv * 512)
                    bpb = buf(f"bank{bank}")
                    lastt = None
                    for blk in range(4):
                        lastt = sch.emit("pe", lambda e, blk=blk, pt=pt:
                                         e.transpose(
                                             pt[:, blk * P:(blk + 1) * P],
                                             self.v_tmp[:, blk * P:
                                                        (blk + 1) * P],
                                             self.ident[:]),
                                         waits=([ev] + (bpb.wait_write()
                                                        if hv == 0 else []))
                                         if blk == 0 else [],
                                         inc="spe" if blk == 3 else None)
                    bpb.set_written(lastt)
                    bv.add_reader(lastt)
                    dst = self.vb(self.O_KV + 32768, 16384).rearrange(
                        "p (t j) -> p t j", j=1024)[
                        :, g * 8 + hv * 4:g * 8 + hv * 4 + 4,
                        kvh * P:(kvh + 1) * P]
                    evt = sch.emit("dve", lambda e, dst=dst, pt=pt:
                                   e.tensor_copy(dst, pt[:].rearrange(
                                       "p (k j) -> p k j", j=P)),
                                   waits=[lastt], inc="sdve")
                    bpb.add_reader(evt)
                    buf(f"v{g}").set_written(evt)

    def attn(self, g, bg=None):
        sch, buf = self.sch, self.buf
        slots = range(4) if g == 0 else range(4, 8)
        for s in slots:
            blocks, masked = SLOTS[s]
            L = len(blocks)
            nch = (L + 7) // 8
            own_tok = (s % 4) * P + (512 if s >= 4 else 0)
            runs = []
            for jj, c in enumerate(blocks):
                if runs and c == runs[-1][1] + runs[-1][2]:
                    runs[-1][2] += 1
                else:
                    runs.append([jj, c, 1])
            for kvh in range(NKVH):
                if self.attn_units is not None and \
                        self._unit_count >= self.attn_units:
                    return
                self._unit_count += 1
                for qh in range(4):
                    h = kvh * 4 + qh
                    ii = s * 32 + kvh * 4 + qh
                    if g == 0:
                        sbase = (ii % 2) * 1024
                        bsc = [buf(f"bank{sbase // 512 + i}") for i in range(2)]
                    else:
                        sbase = 0
                        bsc = [buf(f"bank{b}") for b in range(4)]
                    wait0 = []
                    for b_ in bsc:
                        wait0 += b_.wait_write()
                    wait0 += buf(f"q{g}").wait_read()
                    wait0 += buf("k0").wait_read() + buf("k1").wait_read()
                    first_banks = set()
                    mms = []
                    for (jj0, c0, ln) in runs:
                        col = 0
                        while col < ln * P:
                            pc = jj0 * P + col
                            seg = min(512 - (pc % 512), ln * P - col)
                            bank = pc // 512
                            st = bank not in first_banks
                            first_banks.add(bank)
                            mms.append((pc, c0 * P + col, seg, st))
                            col += seg
                    last = None
                    q_ap = self.vb(self.O_R64 + (h * 1024 + own_tok) * 2, P)
                    for idx, (pc, kc, seg, st) in enumerate(mms):
                        k_ap = self.vb(self.O_KV + kvh * 4096 + kc * 2, seg)
                        last = sch.emit(
                            "pe", lambda e, pc=pc, seg=seg, st=st,
                            q_ap=q_ap, k_ap=k_ap, sbase=sbase:
                            e.matmul(self.ps_h[0:P, sbase + pc:
                                               sbase + pc + seg],
                                     q_ap, k_ap, start=st, stop=True,
                                     skip_group_check=True),
                            waits=wait0 if idx == 0 else [], inc=None)
                    assert masked
                    for mc_i, c in enumerate(masked):
                        jj = blocks.index(c)
                        mi = MASK_INDEX[(s, c)]
                        last = sch.emit(
                            "pe", lambda e, jj=jj, mi=mi, sbase=sbase:
                            e.matmul(self.ps_h[0:P, sbase + jj * P:
                                               sbase + (jj + 1) * P],
                                     self.ident[:],
                                     self.masks_sb[:, mi * P:(mi + 1) * P],
                                     start=False, stop=True,
                                     skip_group_check=True),
                            waits=buf("masks").wait_read(),
                            inc="spe" if mc_i == len(masked) - 1 else None)
                    t_sc = last
                    if getattr(self, "_pend", None) is not None:
                        self._pend()
                        self._pend = None
                    bn = buf("nsm")
                    t_b = sch.emit("dve", lambda e, L=L, ii=ii, sbase=sbase:
                                   e.reduce_max(
                                       self.nsm[:, ii % 2:ii % 2 + 1],
                                       self.ps_h[0:P, sbase:sbase + L * P],
                                       axis=AX.X, negate=True),
                                   waits=[t_sc] + bn.wait_write(),
                                   inc="sdve")
                    bn.set_written(t_b)
                    bp = buf("probs")
                    bl = buf("l_col")
                    t_es = []
                    for ch in range(nch):
                        n = min(8, L - ch * 8) * P
                        w = [t_b]
                        if ch == 0:
                            w += bp.wait_write() + bl.wait_write()
                        t_e = sch.emit("act", lambda e, ch=ch, n=n, ii=ii,
                                       sbase=sbase:
                                       e.activation(
                                           self.probs[:, ch * 1024:
                                                      ch * 1024 + n],
                                           self.ps_h[0:P, sbase + ch * 1024:
                                                     sbase + ch * 1024 + n],
                                           AF.Exp,
                                           bias=self.nsm[:, ii % 2:ii % 2 + 1],
                                           scale=1.0,
                                           accum_out=self.l_part[:, ch:ch + 1]),
                                       waits=w, inc="sact")
                        t_es.append(t_e)
                    for b_ in bsc:
                        b_.add_reader(t_es[-1])
                    bp.set_written(t_es[-1])
                    bn.add_reader(t_es[-1])
                    brc = buf("r_col")
                    if nch == 2:
                        t_l = sch.emit("dve", lambda e, ii=ii: e.tensor_tensor(
                            out=self.l_col[:, ii % 2:ii % 2 + 1],
                            in0=self.l_part[:, 0:1], in1=self.l_part[:, 1:2],
                            op=ALU.add), waits=[t_es[-1]], inc="sdve")
                        bl.set_written(t_l)
                        t_r = sch.emit("dve", lambda e, ii=ii: e.reciprocal(
                            self.r_col[:, ii % 2:ii % 2 + 1],
                            self.l_col[:, ii % 2:ii % 2 + 1]),
                            waits=[t_l] + brc.wait_write(), inc="sdve")
                        bl.add_reader(t_r)
                    else:
                        t_r = sch.emit("dve", lambda e, ii=ii: e.reciprocal(
                            self.r_col[:, ii % 2:ii % 2 + 1],
                            self.l_part[:, 0:1]),
                            waits=[t_es[-1]] + brc.wait_write(), inc="sdve")
                        bl.set_written(t_es[-1])
                        bl.add_reader(t_r)
                    brc.set_written(t_r)
                    pn_toks = []
                    for ch in range(nch):
                        nblk = min(8, L - ch * 8)
                        bpn = buf(f"probs_n{ch}")
                        t_pn = sch.emit("dve", lambda e, ch=ch, nblk=nblk,
                                        ii=ii: e.tensor_scalar(
                                            out=self.probs_n[ch][:, 0:nblk * P],
                                            in0=self.probs[
                                                :, ch * 1024:
                                                ch * 1024 + nblk * P],
                                            scalar1=self.r_col[
                                                :, ii % 2:ii % 2 + 1],
                                            scalar2=None, op0=ALU.mult),
                                        waits=[t_r] + bpn.wait_write(),
                                        inc="sdve")
                        bpn.set_written(t_pn)
                        brc.add_reader(t_pn)
                        bp.add_reader(t_pn)
                        pn_toks.append(t_pn)
                    def deferred(qh=qh, L=L, nch=nch, pn_toks=tuple(pn_toks)):
                        for ch in range(nch):
                            nblk = min(8, L - ch * 8)
                            bank = 4 + (ch % 2)
                            pt = self.psb(bank, 1024)
                            bpb = buf(f"bank{bank}")
                            bpn = buf(f"probs_n{ch}")
                            lastt = None
                            for k in range(nblk):
                                lastt = sch.emit(
                                    "pe", lambda e, k=k, pt=pt, ch=ch:
                                    e.transpose(
                                        pt[:, k * P:(k + 1) * P],
                                        self.probs_n[ch][:, k * P:(k + 1) * P],
                                        self.ident[:]),
                                    waits=([pn_toks[ch]] + bpb.wait_write())
                                    if k == 0 else [],
                                    inc="spe" if k == nblk - 1 else None)
                            bpb.set_written(lastt)
                            bpn.add_reader(lastt)
                            bpt = buf("probsT")
                            dst = self.probsT_all[:].rearrange(
                                "p (jj x) -> p jj x", x=512)[
                                :, ch * 8:ch * 8 + nblk,
                                qh * P:(qh + 1) * P]
                            ev = sch.emit("act", lambda e, dst=dst, pt=pt,
                                          nblk=nblk:
                                          e.copy(dst, pt[:, 0:nblk * P]
                                                 .rearrange(
                                                     "p (k x) -> p k x", x=P)),
                                          waits=[lastt] +
                                          (bpt.wait_write()
                                           if ch == 0 and qh == 0 else []),
                                          inc="sact")
                            bpb.add_reader(ev)
                            bpt.set_written(ev)
                    self._pend = deferred
                if self._pend is not None:
                    self._pend()
                    self._pend = None
                av_bank = 6 + ((s * 8 + kvh) % 2)
                bav = buf(f"bank{av_bank}")
                w0 = bav.wait_write() + buf(f"v{g}").wait_read() + \
                    buf("probsT").wait_read()
                if g == 1:
                    w0 += buf("v0").wait_read()
                last = None
                first = True
                nb = len(blocks)
                for jj, c in enumerate(blocks):
                    v_ap = self.vb(self.O_KV + 32768 + c * 2048 + kvh * 256, P)
                    is_last = (jj == nb - 1)
                    last = sch.emit(
                        "pe", lambda e, jj=jj, v_ap=v_ap,
                        av_bank=av_bank, first=first, is_last=is_last:
                        e.matmul(
                            self.ps_h[0:P, av_bank * 512:
                                      av_bank * 512 + 512],
                            v_ap,
                            self.probsT_all[:, jj * 512:(jj + 1) * 512],
                            start=first, stop=is_last,
                            skip_group_check=True),
                        waits=w0 if first else [],
                        inc="spe" if is_last else None)
                    first = False
                t_av = last
                bav.set_written(t_av)
                buf("probsT").add_reader(t_av)
                dst = self.vb(self.O_R64, 32768).rearrange(
                    "p (hh j) -> p hh j", j=1024)[
                    :, kvh * 4:(kvh + 1) * 4, own_tok:own_tok + P]
                ev = sch.emit("dve", lambda e, dst=dst, av_bank=av_bank:
                              e.tensor_copy(
                                  dst,
                                  self.ps_h[0:P, av_bank * 512:
                                            av_bank * 512 + 512].rearrange(
                                      "p (q j) -> p q j", j=P)),
                              waits=[t_av], inc="sdve")
                bav.add_reader(ev)
                buf(f"attn_o{g}").set_written(ev)
                if bg is not None and (s * 8 + kvh) % 4 == 3:
                    next(bg, None)

    def dense(self):
        sch, buf = self.sch, self.buf
        for og in range(8):
            qwbs = []
            for q in range(2):
                wb = buf(f"dq{q}")
                tok = self.pool_dma(
                    self.dq[q],
                    self.wd_d[og][:, q * 8192:(q + 1) * 8192],
                    waits=wb.wait_write())
                wb.set_written(tok)
                qwbs.append(wb)
            pbs = [buf(f"bank{7 - b}") for b in range(8)]
            last_by_o = [None] * 8
            for fc in range(NFC):
                q = fc // 16
                wb = qwbs[q]
                wcol = (fc % 16) * 512
                for o in range(8):
                    w = []
                    if fc == 0:
                        w += pbs[o].wait_write()
                        w += buf("attn_o0").wait_read() + \
                            buf("attn_o1").wait_read()
                    if fc % 16 == 0:
                        w += wb.wait_read()
                    lhs = self.vb(self.O_R64 + fc * 2048 + o * 256, P)
                    last_by_o[o] = sch.emit(
                        "pe", lambda e, fc=fc, o=o, q=q, wcol=wcol, lhs=lhs:
                        e.matmul(self.psf(7 - o, 512), lhs,
                                 self.dq[q][:, wcol:wcol + 512],
                                 start=(fc == 0), stop=(fc == NFC - 1),
                                 skip_group_check=True),
                        waits=w,
                        inc="spe" if (fc == NFC - 1 or
                                      (fc % 16 == 15 and o == 7)) else None)
                if fc % 16 == 15:
                    wb.add_reader(last_by_o[7])
            xtoks = []
            for gi in range(2):
                bx = buf(f"xs2_{gi}")
                srcv = bass.AP(self.x_d.tensor,
                               gi * 1024 * D + og * 512,
                               [[D, P], [P * D, 4], [1, 512]])
                tokx = self.pool_dma(self.xs2[gi], srcv,
                                     waits=bx.wait_write())
                bx.set_written(tokx)
                xtoks.append((bx, tokx))
            for o in range(8):
                gi, oo = o // 4, o % 4
                bx, tokx = xtoks[gi]
                t_mm = sch.emit("pe", lambda e, o=o, gi=gi, oo=oo: e.matmul(
                    self.psf(7 - o, 512), self.ident[:],
                    self.xs2[gi][:, oo * 512:(oo + 1) * 512],
                    start=False, stop=True, skip_group_check=True),
                    waits=[last_by_o[o], tokx], inc="spe")
                bx.add_reader(t_mm)
                dst = self.vb(self.O_KV + o * 8192 + og * 1024, 512)
                t_ev = sch.emit("dve", lambda e, o=o, dst=dst:
                                e.tensor_copy(dst, self.psf(7 - o, 512)),
                                waits=[t_mm], inc="sdve")
                pbs[o].add_reader(t_ev)
                buf("x1").set_written(t_ev)
        if self.stop_after == "dense_mm":
            return
        sch.fence()
        pend = [None]
        for o in range(8):
            pb_ = o % 2
            x1v = self.vb(self.O_KV + o * 8192, 4096)
            em = sch.chain("dve")
            em(lambda e, x1v=x1v, pb_=pb_: e.tensor_scalar(
                out=self.xh2_tm[pb_], in0=x1v, scalar1=1.0, scalar2=None,
                op0=ALU.mult, op1=ALU.add, accum_out=self.s_col[:]),
                waits=buf(f"xh2_tm{pb_}").wait_write() +
                buf("stat").wait_write())
            em(lambda e, x1v=x1v, pb_=pb_: e.scalar_tensor_tensor(
                out=self.xh2_tm[pb_], in0=x1v, scalar=1.0, in1=x1v,
                op0=ALU.mult, op1=ALU.mult, accum_out=self.q_col[:]))
            em(lambda e: e.tensor_scalar(
                out=self.m_col[:], in0=self.s_col[:], scalar1=1.0 / D,
                scalar2=None, op0=ALU.mult))
            em(lambda e: e.tensor_tensor(
                out=self.s_col[:], in0=self.m_col[:], in1=self.m_col[:],
                op=ALU.mult))
            t_var = em(lambda e: e.scalar_tensor_tensor(
                out=self.var_col[:], in0=self.q_col[:], scalar=1.0 / D,
                in1=self.s_col[:], op0=ALU.mult, op1=ALU.subtract))
            t_ln = sch.emit("act", lambda e: e.activation(
                self.rstd_col[:], self.var_col[:], AF.Ln,
                bias=self.eps_col[:], scale=1.0),
                waits=[t_var] + buf("eps").wait_read(), inc="sact")
            t_rstd = sch.emit("act", lambda e: e.activation(
                self.rstd_col[:], self.rstd_col[:], AF.Exp, bias=0.0,
                scale=-0.5), waits=[t_ln], inc="sact")
            buf("stat").set_written(t_rstd)
            t_n = sch.emit("dve", lambda e, x1v=x1v, pb_=pb_: e.tensor_scalar(
                out=self.xh2_tm[pb_], in0=x1v, scalar1=self.m_col[:],
                scalar2=self.rstd_col[:], op0=ALU.subtract, op1=ALU.mult),
                waits=[t_rstd, t_var], inc="sdve")
            buf(f"xh2_tm{pb_}").set_written(t_n)
            buf("stat").add_reader(t_n)
            buf("x1").add_reader(t_n)
            if pend[0] is not None:
                pend[0]()

            def mk_pend(o=o, pb_=pb_, t_n=t_n):
                def run():
                    for bi in range(4):
                        bank = bi % 2
                        pt = self.psb(bank, 1024)
                        bpb = buf(f"bank{bank}")
                        last = None
                        for k in range(8):
                            col = (bi * 8 + k) * P
                            last = sch.emit(
                                "pe", lambda e, col=col, k=k, pt=pt, pb_=pb_:
                                e.transpose(
                                    pt[:, k * P:(k + 1) * P],
                                    self.xh2_tm[pb_][:, col:col + P],
                                    self.ident[:]),
                                waits=([t_n] + bpb.wait_write())
                                if k == 0 else [],
                                inc="spe" if k == 7 else None)
                        bpb.set_written(last)
                        buf(f"xh2_tm{pb_}").add_reader(last)
                        dst = self.vb(self.O_R64, 32768).rearrange(
                            "p (fc j) -> p fc j", j=1024)[
                            :, bi * 8:(bi + 1) * 8, o * P:(o + 1) * P]
                        src2 = pt[:].rearrange("p (k j) -> p k j", j=P)
                        ev = sch.emit("act", lambda e, dst=dst, src2=src2:
                                      e.copy(dst, src2), waits=[last],
                                      inc="sact")
                        bpb.add_reader(ev)
                        buf("xh2").set_written(ev)
                    self.ffn_passA(o)
                return run
            pend[0] = mk_pend()
        pend[0]()

    def ffn_consts(self):
        sch, buf = self.sch, self.buf
        w0 = buf("xh1_own").wait_write()
        for name, dst, src in (("wr_sb", self.wr_sb, self.wr_d),
                               ("lA_sb", self.lA_sb, self.lA_d),
                               ("lB_sb", self.lB_sb, self.lB_d),
                               ("bb_sb", self.bb_sb, self.bb_d),
                               ("br_sb", self.br_sb[:], self.br_d)):
            tok = self.pool_dma(dst, src, waits=w0)
            buf(name).set_written(tok)
        t_on = sch.emit("pool", lambda e: e.memset(self.ones_col[:], 1.0),
                        inc="spool")
        buf("ones_col").set_written(t_on)
        t_io = sch.emit("pool", lambda e: e.iota(
            self.kiota[:], pattern=[[1, 8]], base=1, channel_multiplier=0,
            allow_small_or_imprecise_dtypes=True),
            inc="spool")
        buf("kiota").set_written(t_io)

    def ffn_passA(self, o):
        sch, buf = self.sch, self.buf
        if True:
            rb_, hb_ = (2, 3) if o % 2 == 0 else (6, 7)
            b0, b1 = buf(f"bank{rb_}"), buf(f"bank{hb_}")
            lasth = None
            for fc in range(NFC):
                w = []
                if fc == 0:
                    w = (b0.wait_write() + b1.wait_write() +
                         buf("xh2").wait_read() + buf("wr_sb").wait_read() +
                         buf("lA_sb").wait_read())
                lhs = self.vb(self.O_R64 + fc * 2048 + o * 256, P)
                sch.emit("pe", lambda e, fc=fc, lhs=lhs, rb_=rb_: e.matmul(
                    self.psf(rb_, 8), lhs, self.wr_sb[:, fc * 8:(fc + 1) * 8],
                    start=(fc == 0), stop=False, skip_group_check=True),
                    waits=w)
                lasth = sch.emit("pe", lambda e, fc=fc, lhs=lhs, hb_=hb_:
                                 e.matmul(
                    self.psf(hb_, P), lhs, self.lA_sb[:, fc * P:(fc + 1) * P],
                    start=(fc == 0), stop=(fc == NFC - 1),
                    skip_group_check=True),
                    inc="spe" if fc == NFC - 1 else None)
            t_rb = sch.emit("pe", lambda e, rb_=rb_: e.matmul(
                self.psf(rb_, 8), self.ones_col[:], self.br_sb[:],
                start=False, stop=True, skip_group_check=True),
                waits=buf("br_sb").wait_read() + buf("ones_col").wait_read(),
                inc="spe")
            b0.set_written(t_rb)
            b1.set_written(lasth)
            em = sch.chain("dve")
            t_z = em(lambda e, rb_=rb_: e.tensor_copy(
                self.zbuf[:], self.psf(rb_, 8)),
                waits=[t_rb] + buf("spx").wait_write())
            b0.add_reader(t_z)
            em(lambda e: e.tensor_copy(self.zs0[:], self.zbuf[:]))
            cur, nxt = self.zs0, self.zs1
            stages = [
                [(0, 1), (2, 3), (4, 5), (6, 7)],
                [(0, 2), (1, 3), (4, 6), (5, 7)],
                [(1, 2), (5, 6)],
                [(0, 4), (1, 5), (2, 6), (3, 7)],
                [(2, 4), (3, 5)],
                [(1, 2), (3, 4), (5, 6)],
            ]
            for st in stages:
                covered = set()
                for (a, b) in st:
                    covered.add(a)
                    covered.add(b)
                    em(lambda e, a=a, b=b, cur=cur, nxt=nxt:
                       e.tensor_tensor(nxt[:, a:a + 1], cur[:, a:a + 1],
                                       cur[:, b:b + 1], op=ALU.max))
                    em(lambda e, a=a, b=b, cur=cur, nxt=nxt:
                       e.tensor_tensor(nxt[:, b:b + 1], cur[:, a:a + 1],
                                       cur[:, b:b + 1], op=ALU.min))
                for j in range(8):
                    if j not in covered:
                        em(lambda e, j=j, cur=cur, nxt=nxt:
                           e.tensor_copy(nxt[:, j:j + 1], cur[:, j:j + 1]))
                cur, nxt = nxt, cur
            zs = cur
            em(lambda e, zs=zs: e.tensor_copy(
                self.cums[:, 0:1], zs[:, 0:1]))
            em(lambda e, zs=zs: e.tensor_tensor(
                self.cums[:, 1:8], zs[:, 1:8], zs[:, 0:7], op=ALU.add))
            em(lambda e, nxt=nxt: e.tensor_copy(
                nxt[:, 0:2], self.cums[:, 0:2]))
            em(lambda e, nxt=nxt: e.tensor_tensor(
                nxt[:, 2:8], self.cums[:, 2:8], self.cums[:, 0:6], op=ALU.add))
            em(lambda e, nxt=nxt: e.tensor_copy(
                self.cums[:, 0:4], nxt[:, 0:4]))
            em(lambda e, nxt=nxt: e.tensor_tensor(
                self.cums[:, 4:8], nxt[:, 4:8], nxt[:, 0:4], op=ALU.add))
            em(lambda e, zs=zs, nxt=nxt: e.tensor_tensor(
                nxt[:], zs[:], self.kiota[:], op=ALU.mult),
                waits=buf("kiota").wait_read())
            em(lambda e: e.tensor_scalar(
                out=self.cums[:], in0=self.cums[:], scalar1=-1.0,
                scalar2=None, op0=ALU.add))
            em(lambda e, nxt=nxt: e.tensor_tensor(
                nxt[:], nxt[:], self.cums[:], op=ALU.is_gt))
            em(lambda e, nxt=nxt: e.reduce_sum(
                self.kz[:, 0:1], nxt[:], axis=AX.X))
            em(lambda e, zs=zs, nxt=nxt: e.scalar_tensor_tensor(
                out=zs[:], in0=nxt[:], scalar=1.0, in1=zs[:],
                op0=ALU.mult, op1=ALU.mult, accum_out=self.kz[:, 1:2]))
            em(lambda e: e.reciprocal(
                self.kz[:, 2:3], self.kz[:, 0:1]))
            em(lambda e: e.tensor_scalar(
                out=self.kz[:, 1:2], in0=self.kz[:, 1:2], scalar1=-1.0,
                scalar2=None, op0=ALU.add))
            em(lambda e: e.tensor_tensor(
                self.kz[:, 3:4], self.kz[:, 1:2], self.kz[:, 2:3],
                op=ALU.mult))
            t_w = em(lambda e: e.tensor_scalar(
                out=self.zbuf[:], in0=self.zbuf[:], scalar1=self.kz[:, 3:4],
                scalar2=0.0, op0=ALU.subtract, op1=ALU.max))
            buf("spx").set_written(t_w)
            bhw = buf("hw")
            t_hw = sch.emit("dve", lambda e, hb_=hb_: e.tensor_tensor(
                self.hw[:].rearrange("p (e r) -> p e r", r=16),
                self.psf(hb_, P).rearrange("p (e r) -> p e r", r=16),
                self.zbuf[:, :, None].broadcast_to([P, 8, 16]),
                op=ALU.mult),
                waits=[t_w, lasth] + bhw.wait_write(), inc="sdve")
            b1.add_reader(t_hw)
            bhw.set_written(t_hw)
            bpb = buf("bank5")
            pt = self.psb(5, 1024)
            t_tr = sch.emit("pe", lambda e, pt=pt: e.transpose(
                pt[:, 0:P], self.hw[:], self.ident[:]),
                waits=[t_hw] + bpb.wait_write(), inc="spe")
            bpb.set_written(t_tr)
            bhw.add_reader(t_tr)
            t_ht = sch.emit("dve", lambda e, o=o, pt=pt: e.tensor_copy(
                self.hwT[:, o * P:(o + 1) * P], pt[:, 0:P]),
                waits=[t_tr] + buf("h_wT").wait_write(), inc="sdve")
            bpb.add_reader(t_ht)
            buf("h_wT").set_written(t_ht)

    def ffn(self):
        sch, buf = self.sch, self.buf
        for og in range(8):
            qwbs = []
            for q in range(2):
                wb = buf(f"dq{q}")
                tok = self.pool_dma(
                    self.dq[q],
                    self.wb_d[og][:, q * 8192:(q + 1) * 8192],
                    waits=wb.wait_write())
                wb.set_written(tok)
                qwbs.append(wb)
            pbs = [buf(f"bank{b}") for b in range(8)]
            last_by_o = [None] * 8
            for fc in range(NFC):
                q = fc // 16
                wb = qwbs[q]
                wcol = (fc % 16) * 512
                for o in range(8):
                    w = []
                    if fc == 0:
                        w += pbs[o].wait_write() + buf("xh2").wait_read()
                    if fc % 16 == 0:
                        w += wb.wait_read()
                    lhs = self.vb(self.O_R64 + fc * 2048 + o * 256, P)
                    last_by_o[o] = sch.emit(
                        "pe", lambda e, fc=fc, o=o, q=q, wcol=wcol, lhs=lhs:
                        e.matmul(self.psf(o, 512), lhs,
                                 self.dq[q][:, wcol:wcol + 512],
                                 start=(fc == 0), stop=False,
                                 skip_group_check=True),
                        waits=w,
                        inc="spe" if (fc == NFC - 1 or
                                      (fc % 16 == 15 and o == 7)) else None)
                if fc % 16 == 15:
                    wb.add_reader(last_by_o[7])
            for o in range(8):
                sch.emit("pe", lambda e, o=o, og=og: e.matmul(
                    self.psf(o, 512),
                    self.hwT[:, o * P:(o + 1) * P],
                    self.lB_sb[:, og * 512:(og + 1) * 512],
                    start=False, stop=False, skip_group_check=True),
                    waits=buf("h_wT").wait_read() + buf("lB_sb").wait_read())
                last = sch.emit("pe", lambda e, o=o, og=og: e.matmul(
                    self.psf(o, 512), self.ones_col[:],
                    self.bb_sb[0:1, og * 512:(og + 1) * 512],
                    start=False, stop=True, skip_group_check=True),
                    waits=buf("bb_sb").wait_read(), inc="spe")
                x1v = self.vb(self.O_KV + o * 8192 + og * 1024, 512)
                last = sch.emit("pe", lambda e, o=o, x1v=x1v: e.matmul(
                    self.psf(o, 512), self.ident[:], x1v,
                    start=False, stop=True, skip_group_check=True),
                    waits=[last] + buf("x1").wait_read(), inc="spe")
                pbs[o].set_written(last)
                bo = buf(f"op{o % 2}")
                t_ev = sch.emit("dve", lambda e, o=o:
                                e.tensor_copy(self.op_[o % 2],
                                              self.psf(o, 512)),
                                waits=[last] + bo.wait_write(), inc="sdve")
                pbs[o].add_reader(t_ev)
                bo.set_written(t_ev)
                t_st = sch.emit("sync", lambda e, o=o, og=og: e.dma_start(
                    out=self.y_d[o * P:(o + 1) * P, og * 512:(og + 1) * 512],
                    in_=self.op_[o % 2]),
                    waits=[t_ev], inc="sst", amt=16)
                bo.add_reader(t_st)


def build_masks(p):
    m = np.zeros((NMASK, P, P), dtype=np.float32)
    tri = np.triu(np.full((P, P), NEG, dtype=np.float32), k=1)
    for cls, mi in _PAIRS.items():
        c = cls[p]
        if c == "tri":
            m[mi] = tri
        elif c == "neg":
            m[mi] = NEG
    return np.ascontiguousarray(
        m.transpose(1, 0, 2).reshape(P, NMASK * P)).astype(ml_dtypes.bfloat16)


def _pack_weights(inputs):
    d = {}
    d["wqkv_p"] = np.ascontiguousarray(
        np.asarray(inputs["L0_attn_w_qkv"], dtype=np.float32)
        .reshape(48, 128, 32, 128).transpose(0, 3, 2, 1).reshape(48, 128, 4096))
    d["wd_p"] = np.ascontiguousarray(
        np.asarray(inputs["L0_attn_w_dense"], dtype=np.float32)
        .reshape(8, 512, 32, 128).transpose(0, 3, 2, 1).reshape(8, 128, 16384))
    d["wb_p"] = np.ascontiguousarray(
        np.asarray(inputs["L0_ffn_W_base"], dtype=np.float32)
        .reshape(8, 512, 32, 128).transpose(0, 3, 2, 1).reshape(8, 128, 16384))
    d["wr_p"] = np.ascontiguousarray(
        np.asarray(inputs["L0_ffn_W_router"], dtype=np.float32)
        .T.reshape(32, 128, 8).transpose(1, 0, 2).reshape(128, 256))
    d["lA_p"] = np.ascontiguousarray(
        np.asarray(inputs["L0_ffn_lora_A"], dtype=np.float32)
        .transpose(1, 0, 2).reshape(32, 128, 128).transpose(1, 0, 2)
        .reshape(128, 4096))
    d["lB_p"] = np.ascontiguousarray(
        np.asarray(inputs["L0_ffn_lora_B"], dtype=np.float32).reshape(128, 4096))
    d["bb"] = np.ascontiguousarray(
        np.asarray(inputs["L0_ffn_b_base"], dtype=np.float32).reshape(1, D))
    d["br"] = np.ascontiguousarray(
        np.asarray(inputs["L0_ffn_b_router"], dtype=np.float32).reshape(1, 8))
    return d


def kernel(**inputs):
    x = np.asarray(inputs["L0_attn_q"], dtype=np.float32)
    shared = _pack_weights(inputs)
    kb = KB()
    nc = kb.build()
    in_maps = []
    for c in range(8):
        b, p = c // 2, c % 2
        perm = np.concatenate(
            [np.arange(t * P, (t + 1) * P) for t in col_order(p)])
        m = dict(shared)
        m["x_perm"] = np.ascontiguousarray(x[b][perm])
        m["masks"] = build_masks(p)
        in_maps.append(m)
    res = run_bass_kernel_spmd(nc, in_maps, core_ids=list(range(8)))
    y = np.empty((B, S, D), dtype=np.float32)
    for c in range(8):
        b, p = c // 2, c % 2
        out = res.results[c]["y_out"]
        for o, t in enumerate(T_SETS[p]):
            y[b, t * P:(t + 1) * P, :] = out[o * P:(o + 1) * P, :]
    return y


# version 21 (final)
# speedup vs baseline: 1.0226x; 1.0207x over previous
import math
import numpy as np
import ml_dtypes
import concourse.bass as bass
import concourse.mybir as mybir
from concourse.bass_utils import run_bass_kernel_spmd

F32 = mybir.dt.float32
BF16 = mybir.dt.bfloat16
AF = mybir.ActivationFunctionType
ALU = mybir.AluOpType
AX = mybir.AxisListType

B, S, D = 4, 2048, 4096
P = 128
NFC = 32
NKVH = 8
SCALE = 1.0 / math.sqrt(128.0)
EPS = 1e-5
NEG = -30000.0

T_SETS = ([0, 3, 4, 7, 8, 11, 12, 15], [1, 2, 5, 6, 9, 10, 13, 14])
OWN_ROW = [0, 1, 2, 3, 8, 9, 10, 11]


def col_order(p):
    own, oth = T_SETS[p], T_SETS[1 - p]
    return own[0:4] + oth[0:4] + own[4:8] + oth[4:8]


def attn_structure():
    cols = {p: col_order(p) for p in (0, 1)}
    slots = []
    mask_index = {}
    mi = 0
    for s in range(8):
        cq = s if s < 4 else s + 4
        blocks = []
        for sec in range(4):
            base = sec * 4
            pref = 0
            for p in (0, 1):
                gq = cols[p][cq]
                n = sum(1 for k in range(4) if cols[p][base + k] <= gq)
                pref = max(pref, n)
            blocks += [base + k for k in range(pref)]
        masked = [c for c in blocks
                  if any(cols[p][c] >= cols[p][cq] for p in (0, 1))]
        for c in masked:
            mask_index[(s, c)] = mi
            mi += 1
        slots.append((blocks, masked))
    return cols, slots, mask_index, mi


COLS, SLOTS, _MASK_INDEX_FULL, _NMASK_FULL = attn_structure()
_PAIRS = {}
MASK_INDEX = {}
for _s in range(8):
    _cq = _s if _s < 4 else _s + 4
    for _c in SLOTS[_s][1]:
        _cls = tuple(
            ("tri" if COLS[_p][_c] == COLS[_p][_cq] else
             ("neg" if COLS[_p][_c] > COLS[_p][_cq] else "zero"))
            for _p in (0, 1))
        if _cls not in _PAIRS:
            _PAIRS[_cls] = len(_PAIRS)
        MASK_INDEX[(_s, _c)] = _PAIRS[_cls]
NMASK = len(_PAIRS)

ENGS = ("pe", "dve", "act", "pool", "sync")
SEM_ENG = {"spe": "pe", "sdve": "dve", "sact": "act", "spool": "pool"}


class Sched:
    def __init__(self):
        self.streams = {e: [] for e in ENGS}
        self.cnt = {}
        self.waited = {e: {} for e in ENGS}
        self.sems = {}
        self.guards = []

    def new_sem(self, nc, name):
        g = nc.semaphore(name)
        self.sems[name] = g.__enter__()
        self.guards.append(g)
        self.cnt[name] = 0

    def wait(self, eng, tok):
        if tok is None:
            return
        name, val = tok
        w = self.waited[eng]
        if w.get(name, 0) >= val:
            return
        w[name] = val
        h = self.sems[name]
        self.streams[eng].append(lambda e, h=h, v=val: e.wait_ge(h, v))

    def emit(self, eng, fn, waits=(), inc=None, amt=1):
        for t in waits:
            self.wait(eng, t)
        if inc is None:
            self.streams[eng].append(fn)
            return None
        self.cnt[inc] += amt
        tok = (inc, self.cnt[inc])
        h = self.sems[inc]
        self.streams[eng].append(
            lambda e, fn=fn, h=h, a=amt: fn(e).then_inc(h, a))
        return tok

    def chain(self, eng):
        sem = {"dve": "sdve", "act": "sact", "pool": "spool",
               "pe": "spe"}[eng]
        state = {"tok": None}

        def em(fn, waits=()):
            w = list(waits)
            if state["tok"] is not None:
                w.append(state["tok"])
            t = self.emit(eng, fn, waits=w, inc=sem)
            state["tok"] = t
            return t
        return em

    def fence(self):
        toks = []
        fs = self.fence_scratch
        ident = self.fence_ident
        ops = {"pe": lambda en: en.ldweights(ident[:]),
               "dve": lambda en: en.memset(fs[0:1, 0:1], 0.0),
               "act": lambda en: en.activation(fs[0:1, 0:1], fs[0:1, 1:2],
                                               AF.Identity, bias=0.0,
                                               scale=0.0),
               "pool": lambda en: en.memset(fs[0:1, 1:2], 0.0)}
        for e in ("pe", "dve", "act", "pool"):
            sem = "f_" + e
            self.cnt[sem] += 1
            toks.append((sem, self.cnt[sem]))
            h = self.sems[sem]
            op = ops[e]
            self.streams[e].append(lambda en, h=h, op=op: op(en).then_inc(h, 1))
        toks.append(("sx", self.cnt["sx"]))
        toks.append(("sw", self.cnt["sw"]))
        for e in ENGS:
            for t in toks:
                self.wait(e, t)


class Buf:
    def __init__(self):
        self.w = {}
        self.r = {}

    def wait_write(self):
        return list(self.w.items()) + list(self.r.items())

    def wait_read(self):
        return list(self.w.items())

    def set_written(self, tok):
        if tok is None:
            return
        s, v = tok
        self.w[s] = max(self.w.get(s, 0), v)

    def add_reader(self, tok):
        if tok is None:
            return
        s, v = tok
        self.r[s] = max(self.r.get(s, 0), v)


class KB:
    def __init__(self, stop_after=None, taps=(), attn_units=None,
                 attn_stage=99):
        self.stop_after = stop_after
        self.tap_list = list(taps)
        self.attn_units = attn_units
        self.attn_stage = attn_stage
        self._unit_count = 0
        self._pend = None
        self.nc = bass.Bass("TRN2", target_bir_lowering=False,
                            num_swdge_queues=4,
                            detect_race_conditions=False)
        self.sch = Sched()
        self.bufs = {}
        self.guards = []
        self.pool_dma_toks = []

    def buf(self, name):
        if name not in self.bufs:
            self.bufs[name] = Buf()
        return self.bufs[name]

    def sb(self, name, shape, dt):
        g = self.nc.sbuf_tensor(name, shape, dt)
        t = g.__enter__()
        self.guards.append(g)
        return t

    def pool_dma(self, dst, src, waits=()):
        w = list(waits)
        if len(self.pool_dma_toks) >= 4:
            w.append(self.pool_dma_toks[-4])
        tok = self.sch.emit("pool", lambda e, d=dst, s=src:
                            e.dma_start(out=d, in_=s),
                            waits=w, inc="sw", amt=16)
        self.pool_dma_toks.append(tok)
        return tok

    def build(self):
        nc, sch = self.nc, self.sch
        for s in ("sx", "sw", "spe", "sact", "sdve", "spool", "sst",
                  "f_pe", "f_dve", "f_act", "f_pool"):
            sch.new_sem(nc, s)

        self.x_d = nc.dram_tensor("x_perm", [S, D], F32, kind="ExternalInput").ap()
        self.mask_d = nc.dram_tensor("masks", [P, NMASK * P], BF16,
                                     kind="ExternalInput").ap()
        self.wqkv_d = nc.dram_tensor("wqkv_p", [48, P, 4096], F32,
                                     kind="ExternalInput").ap()
        self.wd_d = nc.dram_tensor("wd_p", [8, P, 16384], F32,
                                   kind="ExternalInput").ap()
        self.wb_d = nc.dram_tensor("wb_p", [8, P, 16384], F32,
                                   kind="ExternalInput").ap()
        self.wr_d = nc.dram_tensor("wr_p", [P, 256], F32,
                                   kind="ExternalInput").ap()
        self.lA_d = nc.dram_tensor("lA_p", [P, 4096], F32,
                                   kind="ExternalInput").ap()
        self.lB_d = nc.dram_tensor("lB_p", [P, 4096], F32,
                                   kind="ExternalInput").ap()
        self.bb_d = nc.dram_tensor("bb", [1, D], F32, kind="ExternalInput").ap()
        self.br_d = nc.dram_tensor("br", [1, 8], F32, kind="ExternalInput").ap()
        self.y_d = nc.dram_tensor("y_out", [1024, D], F32,
                                  kind="ExternalOutput").ap()

        AR_F32 = 52480
        g = nc.sbuf_tensor("arena", [P, AR_F32], F32)
        self.arena = g.__enter__()
        self.guards.append(g)
        self.ar_h = self.arena.ap().tensor
        self.ar_bf = self.ar_h.bitcast(BF16)

        self.O_REGA = 0
        self.O_R64 = 32768
        self.O_KV = 98304
        self.O_D = 163840
        self.O_E = 176128

        self.xp = [self.vf(self.O_E, 512), self.vf(self.O_E + 2048, 512)]
        self.xh1_tm = self.vb(self.O_E + 4096, 2048)
        self.v_tmp = self.vb(self.O_E + 8192, 512)
        self.probs = self.vb(self.O_E + 9216, 2048)
        self.probs_n = [self.vb(self.O_E + 13312, 1024),
                        self.vb(self.O_E + 15360, 1024)]
        self.probsT_all = self.vb(self.O_E + 17408, 8192)
        self.wbuf = [self.vb(self.O_D + i * 4096, 2048) for i in range(3)]
        self.dq = [self.vb(self.O_E + i * 16384, 8192) for i in range(2)]
        self.xs2 = [self.vb(self.O_D, 2048), self.vb(self.O_D + 4096, 2048)]
        self.xh2_tm = [self.vb(self.O_E, 4096), self.vb(self.O_E + 8192, 4096)]
        self.lA_sb = self.vb(self.O_REGA, 4096)
        self.lB_sb = self.vb(self.O_REGA + 8192, 4096)
        self.bb_sb = self.vb(self.O_REGA + 16384, 4096, rows=1)
        self.wr_sb = self.vb(self.O_REGA + 24576, 256)
        self.hwT = self.vb(self.O_REGA + 25600, 1024)
        self.op_ = [self.vf(self.O_D, 512), self.vf(self.O_D + 2048, 512)]

        sb = self.sb
        self.ident = sb("ident", [P, P], BF16)
        self.masks_sb = sb("masks_sb", [P, NMASK * P], BF16)
        self.sum_part = sb("sum_part", [P, 8], F32)
        self.sumsq_part = sb("sumsq_part", [P, 8], F32)
        self.s_col = sb("s_col", [P, 1], F32)
        self.q_col = sb("q_col", [P, 1], F32)
        self.m_col = sb("m_col", [P, 1], F32)
        self.var_col = sb("var_col", [P, 1], F32)
        self.rstd_col = sb("rstd_col", [P, 1], F32)
        self.nsm = sb("nsm", [P, 2], F32)
        self.l_part = sb("l_part", [P, 2], F32)
        self.l_col = sb("l_col", [P, 2], F32)
        self.r_col = sb("r_col", [P, 2], F32)
        self.zbuf = sb("zbuf", [P, 8], F32)
        self.zs0 = sb("zs0", [P, 8], F32)
        self.zs1 = sb("zs1", [P, 8], F32)
        self.cums = sb("cums", [P, 8], F32)
        self.kz = sb("kz", [P, 4], F32)
        self.kiota = sb("kiota", [P, 8], F32)
        self.ones_col = sb("ones_col", [1, P], BF16)
        self.br_sb = sb("br_sb", [1, 8], BF16)
        self.hw = sb("hw", [P, P], BF16)
        self.eps_col = sb("eps_col", [P, 1], F32)
        self.fence_scratch = sb("fence_scratch", [1, 2], F32)
        sch.fence_scratch = self.fence_scratch
        sch.fence_ident = self.ident

        g = nc.psum_tensor("psA", [P, 4096], F32)
        self.psum = g.__enter__()
        self.guards.append(g)
        self.ps_h = self.psum.ap().tensor
        self.ps_bf = self.ps_h.bitcast(BF16)

        def attn0_bg():
            bg = self.p1_gen(1)
            self.attn(0, bg=bg)
            for _ in bg:
                pass
        phases = [("p0", lambda: self.p0()),
                  ("p1g0", lambda: self.p1(0)),
                  ("p2g0", lambda: self.p2(0)),
                  ("attn0", attn0_bg),
                  ("p1g1", lambda: None),
                  ("p2g1", lambda: self.p2(1)),
                  ("consts", lambda: self.ffn_consts()),
                  ("attn1", lambda: self.attn(1)),
                  ("fence1", lambda: sch.fence()),
                  ("dense", lambda: self.dense()),
                  ("fence2", lambda: sch.fence()),
                  ("ffn", lambda: self.ffn())]
        for name, fn in phases:
            fn()
            if self.stop_after == name:
                break
        if self.tap_list:
            sch.fence()
            for name, off, cols in self.tap_list:
                tap_d = nc.dram_tensor(name, [P, cols], F32,
                                       kind="ExternalOutput").ap()
                if isinstance(off, str):
                    src_ap = getattr(self, off)[:] if cols is None else \
                        getattr(self, off)[:, 0:cols]
                else:
                    src_ap = self.vf(off, cols)
                sch.emit("sync", lambda e, d=tap_d, s=src_ap:
                         e.dma_start(out=d, in_=s), inc="sst", amt=16)
        sch.wait("sync", ("sst", sch.cnt["sst"]))

        with nc.Block() as block:
            block.tensor(lambda e: [f(e) for f in sch.streams["pe"]])
            block.vector(lambda e: [f(e) for f in sch.streams["dve"]])
            block.scalar(lambda e: [f(e) for f in sch.streams["act"]])
            block.gpsimd(lambda e: [f(e) for f in sch.streams["pool"]])
            block.sync(lambda e: [f(e) for f in sch.streams["sync"]])

        for g in reversed(self.guards):
            g.__exit__(None, None, None)
        return nc

    def vf(self, off, cols, rows=P):
        return self.ar_h[0:rows, off // 4: off // 4 + cols]

    def vb(self, off, cols, rows=P):
        return self.ar_bf[0:rows, off // 2: off // 2 + cols]

    def psf(self, bank, cols, off=0):
        c0 = bank * 512 + off
        return self.ps_h[0:P, c0:c0 + cols]

    def psb(self, bank, cols, off=0):
        c0 = bank * 1024 + off
        return self.ps_bf[0:P, c0:c0 + cols]

    def p0(self):
        sch, buf = self.sch, self.buf
        t0 = sch.emit("pool", lambda e: e.memset(self.ident[:], 0.0),
                      inc="spool")
        t = sch.emit("pool", lambda e: e.affine_select(
            out=self.ident[:], in_=self.ident[:], pattern=[[-1, P]],
            compare_op=ALU.not_equal, fill=1.0, base=0, channel_multiplier=1),
            waits=[t0], inc="spool")
        buf("ident").set_written(t)
        t = sch.emit("sync", lambda e: e.dma_start(
            out=self.masks_sb[:], in_=self.mask_d), inc="sx", amt=16)
        buf("masks").set_written(t)
        t = sch.emit("pool", lambda e: e.memset(self.eps_col[:], EPS),
                     inc="spool")
        buf("eps").set_written(t)

    def p1(self, g):
        for _ in self.p1_gen(g):
            pass

    def p1_gen(self, g):
        sch, buf = self.sch, self.buf
        for c in range(8):
            row0 = (g * 8 + c) * P
            t2 = None
            for pc in range(8):
                bx = buf(f"xp{pc % 2}")
                tok = sch.emit("sync", lambda e, pc=pc, row0=row0: e.dma_start(
                    out=self.xp[pc % 2],
                    in_=self.x_d[row0:row0 + P, pc * 512:(pc + 1) * 512]),
                    waits=bx.wait_write(), inc="sx", amt=16)
                bx.set_written(tok)
                w1 = bx.wait_read()
                if pc == 0:
                    w1 = w1 + buf("xh1_tm").wait_write()
                sch.emit("dve", lambda e, pc=pc: e.tensor_scalar(
                    out=self.xh1_tm[:, 0:512], in0=self.xp[pc % 2],
                    scalar1=1.0, scalar2=None, op0=ALU.mult, op1=ALU.add,
                    accum_out=self.sum_part[:, pc:pc + 1]), waits=w1)
                t2 = sch.emit("dve", lambda e, pc=pc: e.scalar_tensor_tensor(
                    out=self.xh1_tm[:, 512:1024], in0=self.xp[pc % 2],
                    scalar=1.0, in1=self.xp[pc % 2],
                    op0=ALU.mult, op1=ALU.mult,
                    accum_out=self.sumsq_part[:, pc:pc + 1]),
                    inc="sdve")
                bx.add_reader(t2)
            em = sch.chain("dve")
            em(lambda e: e.reduce_sum(
                self.s_col[:], self.sum_part[:], axis=AX.X),
                waits=buf("stat").wait_write() + [t2])
            em(lambda e: e.reduce_sum(
                self.q_col[:], self.sumsq_part[:], axis=AX.X))
            em(lambda e: e.tensor_scalar(
                out=self.m_col[:], in0=self.s_col[:], scalar1=1.0 / D,
                scalar2=None, op0=ALU.mult))
            em(lambda e: e.tensor_tensor(
                out=self.s_col[:], in0=self.m_col[:], in1=self.m_col[:],
                op=ALU.mult))
            t_var = em(lambda e: e.scalar_tensor_tensor(
                out=self.var_col[:], in0=self.q_col[:], scalar=1.0 / D,
                in1=self.s_col[:], op0=ALU.mult, op1=ALU.subtract))
            t_ln = sch.emit("act", lambda e: e.activation(
                self.rstd_col[:], self.var_col[:], AF.Ln,
                bias=self.eps_col[:], scale=1.0),
                waits=[t_var] + buf("eps").wait_read(), inc="sact")
            t_rstd = sch.emit("act", lambda e: e.activation(
                self.rstd_col[:], self.rstd_col[:], AF.Exp, bias=0.0,
                scale=-0.5), waits=[t_ln], inc="sact")
            buf("stat").set_written(t_rstd)
            for half in range(2):
                t_norm = None
                for k4 in range(4):
                    pc = half * 4 + k4
                    bx = buf(f"xp{pc % 2}")
                    tok = sch.emit("sync", lambda e, pc=pc, row0=row0:
                                   e.dma_start(
                                       out=self.xp[pc % 2],
                                       in_=self.x_d[row0:row0 + P,
                                                    pc * 512:(pc + 1) * 512]),
                                   waits=bx.wait_write(), inc="sx", amt=16)
                    bx.set_written(tok)
                    w1 = bx.wait_read() + [t_rstd, t_var]
                    if k4 == 0:
                        w1 = w1 + buf("xh1_tm").wait_write()
                    t_norm = sch.emit("dve", lambda e, pc=pc, k4=k4:
                                      e.tensor_scalar(
                                          out=self.xh1_tm[:, k4 * 512:
                                                          (k4 + 1) * 512],
                                          in0=self.xp[pc % 2],
                                          scalar1=self.m_col[:],
                                          scalar2=self.rstd_col[:],
                                          op0=ALU.subtract, op1=ALU.mult),
                                      waits=w1,
                                      inc="sdve")
                    bx.add_reader(t_norm)
                buf("xh1_tm").set_written(t_norm)
                buf("stat").add_reader(t_norm)
                for bi in range(2):
                    bank = 4 + bi
                    pt = self.psb(bank, 1024)
                    bpb = buf(f"bank{bank}")
                    last = None
                    for k in range(8):
                        col = (bi * 8 + k) * P
                        last = sch.emit("pe", lambda e, col=col, k=k, pt=pt:
                                        e.transpose(
                                            pt[:, k * P:(k + 1) * P],
                                            self.xh1_tm[:, col:col + P],
                                            self.ident[:]),
                                        waits=([t_norm] + bpb.wait_write() +
                                               buf("ident").wait_read())
                                        if k == 0 else [],
                                        inc="spe" if k == 7 else None)
                    bpb.set_written(last)
                    buf("xh1_tm").add_reader(last)
                    fc0 = half * 16 + bi * 8
                    if c < 4:
                        dst = self.vb(self.O_REGA, 16384).rearrange(
                            "p (fc j) -> p fc j", j=512)[
                            :, fc0:fc0 + 8, c * P:(c + 1) * P]
                        bname = "xh1_own"
                    else:
                        dst = self.vb(self.O_R64, 32768).rearrange(
                            "p (fc j) -> p fc j", j=1024)[
                            :, fc0:fc0 + 8,
                            512 + (c - 4) * P: 512 + (c - 3) * P]
                        bname = "xh1_oth"
                    src = pt[:].rearrange("p (k j) -> p k j", j=P)
                    if bi == 0:
                        ev = sch.emit("act", lambda e, dst=dst, src=src:
                                      e.copy(dst, src),
                                      waits=[last] + buf(bname).wait_write(),
                                      inc="sact")
                    else:
                        ev = sch.emit("dve", lambda e, dst=dst, src=src:
                                      e.tensor_copy(dst, src),
                                      waits=[last] + buf(bname).wait_write(),
                                      inc="sdve")
                    bpb.add_reader(ev)
                    buf(bname).set_written(ev)
            yield c

    def p2(self, g):
        sch, buf = self.sch, self.buf
        if g == 0:
            oc_order = list(range(32)) + list(range(32, 40)) + \
                list(range(40, 48))
        else:
            oc_order = list(range(32, 40)) + list(range(40, 48)) + \
                list(range(32))
        si = 0
        for oc in oc_order:
            is_q = oc < 32
            halves = []
            for h in range(2):
                wb = buf(f"wbuf{si % 3}")
                tok = self.pool_dma(
                    self.wbuf[si % 3],
                    self.wqkv_d[oc][:, h * 2048:(h + 1) * 2048],
                    waits=wb.wait_write())
                wb.set_written(tok)
                halves.append((self.wbuf[si % 3], wb))
                si += 1
            if is_q:
                banks = [6 + (oc % 2)]
                npiece = 1
            else:
                banks = [2 + 2 * (oc % 2), 3 + 2 * (oc % 2)]
                npiece = 2
            pb = [buf(f"bank{b}") for b in banks]
            last = None
            for fc in range(NFC):
                wsb, wbb = halves[fc // 16]
                wcol = (fc % 16) * P
                for piece in range(npiece):
                    if is_q or piece == 0:
                        rhs = self.vb(self.O_REGA, 16384).rearrange(
                            "p (fc j) -> p fc j", j=512)[:, fc, :]
                    else:
                        rhs = self.vb(self.O_R64, 32768).rearrange(
                            "p (fc j) -> p fc j", j=1024)[:, fc, 512:1024]
                    bank = banks[piece]
                    w = []
                    if fc == 0 and piece == 0:
                        for b_ in pb:
                            w += b_.wait_write()
                        w += buf("xh1_own").wait_read()
                        if not is_q:
                            w += buf("xh1_oth").wait_read()
                    if fc % 16 == 0 and piece == 0:
                        w = w + wbb.wait_read()
                    last = sch.emit("pe", lambda e, wsb=wsb, wcol=wcol,
                                    bank=bank, rhs=rhs, fc=fc:
                                    e.matmul(self.psf(bank, 512),
                                             wsb[:, wcol:wcol + P], rhs,
                                             start=(fc == 0),
                                             stop=(fc == NFC - 1),
                                             skip_group_check=True),
                                    waits=w,
                                    inc="spe" if (fc == NFC - 1 and
                                                  piece == npiece - 1)
                                    else None)
            for h_ in halves:
                h_[1].add_reader(last)
            buf("xh1_own").add_reader(last)
            if not is_q:
                buf("xh1_oth").add_reader(last)
            for b_ in pb:
                b_.set_written(last)
            if is_q:
                dst = self.vb(self.O_R64 + oc * 2048 + g * 1024, 512)
                ev = sch.emit("act", lambda e, dst=dst, b0=banks[0]:
                              e.activation(dst, self.psf(b0, 512), AF.Copy,
                                           bias=0.0, scale=SCALE),
                              waits=[last], inc="sact")
                pb[0].add_reader(ev)
                buf(f"q{g}").set_written(ev)
            elif oc < 40:
                kvh = oc - 32
                dst = self.vb(self.O_KV + kvh * 4096 + g * 2048, 1024)
                ev = sch.emit("act", lambda e, dst=dst, b0=banks[0]:
                              e.copy(dst, self.psf(b0, 1024)),
                              waits=[last], inc="sact")
                pb[0].add_reader(ev)
                pb[1].add_reader(ev)
                buf(f"k{g}").set_written(ev)
            else:
                kvh = oc - 40
                for hv in range(2):
                    bv = buf("v_tmp")
                    ev = sch.emit("act", lambda e, b0=banks[0], hv=hv:
                                  e.copy(self.v_tmp,
                                         self.psf(b0, 512, off=hv * 512)),
                                  waits=[last] + bv.wait_write(), inc="sact")
                    bv.set_written(ev)
                    pb[0].add_reader(ev)
                    pb[1].add_reader(ev)
                    bank = oc % 2
                    pt = self.psb(bank, 512, off=hv * 512)
                    bpb = buf(f"bank{bank}")
                    lastt = None
                    for blk in range(4):
                        lastt = sch.emit("pe", lambda e, blk=blk, pt=pt:
                                         e.transpose(
                                             pt[:, blk * P:(blk + 1) * P],
                                             self.v_tmp[:, blk * P:
                                                        (blk + 1) * P],
                                             self.ident[:]),
                                         waits=([ev] + (bpb.wait_write()
                                                        if hv == 0 else []))
                                         if blk == 0 else [],
                                         inc="spe" if blk == 3 else None)
                    bpb.set_written(lastt)
                    bv.add_reader(lastt)
                    dst = self.vb(self.O_KV + 32768, 16384).rearrange(
                        "p (t j) -> p t j", j=1024)[
                        :, g * 8 + hv * 4:g * 8 + hv * 4 + 4,
                        kvh * P:(kvh + 1) * P]
                    evt = sch.emit("dve", lambda e, dst=dst, pt=pt:
                                   e.tensor_copy(dst, pt[:].rearrange(
                                       "p (k j) -> p k j", j=P)),
                                   waits=[lastt], inc="sdve")
                    bpb.add_reader(evt)
                    buf(f"v{g}").set_written(evt)

    def attn(self, g, bg=None):
        sch, buf = self.sch, self.buf
        slots = range(4) if g == 0 else range(4, 8)
        for s in slots:
            blocks, masked = SLOTS[s]
            L = len(blocks)
            nch = (L + 7) // 8
            own_tok = (s % 4) * P + (512 if s >= 4 else 0)
            runs = []
            for jj, c in enumerate(blocks):
                if runs and c == runs[-1][1] + runs[-1][2]:
                    runs[-1][2] += 1
                else:
                    runs.append([jj, c, 1])
            for kvh in range(NKVH):
                if self.attn_units is not None and \
                        self._unit_count >= self.attn_units:
                    return
                self._unit_count += 1
                for qh in range(4):
                    h = kvh * 4 + qh
                    ii = s * 32 + kvh * 4 + qh
                    if g == 0:
                        sbase = (ii % 2) * 1024
                        bsc = [buf(f"bank{sbase // 512 + i}") for i in range(2)]
                    else:
                        sbase = 0
                        bsc = [buf(f"bank{b}") for b in range(4)]
                    wait0 = []
                    for b_ in bsc:
                        wait0 += b_.wait_write()
                    wait0 += buf(f"q{g}").wait_read()
                    wait0 += buf("k0").wait_read() + buf("k1").wait_read()
                    first_banks = set()
                    mms = []
                    for (jj0, c0, ln) in runs:
                        col = 0
                        while col < ln * P:
                            pc = jj0 * P + col
                            seg = min(512 - (pc % 512), ln * P - col)
                            bank = pc // 512
                            st = bank not in first_banks
                            first_banks.add(bank)
                            mms.append((pc, c0 * P + col, seg, st))
                            col += seg
                    last = None
                    q_ap = self.vb(self.O_R64 + (h * 1024 + own_tok) * 2, P)
                    for idx, (pc, kc, seg, st) in enumerate(mms):
                        k_ap = self.vb(self.O_KV + kvh * 4096 + kc * 2, seg)
                        last = sch.emit(
                            "pe", lambda e, pc=pc, seg=seg, st=st,
                            q_ap=q_ap, k_ap=k_ap, sbase=sbase:
                            e.matmul(self.ps_h[0:P, sbase + pc:
                                               sbase + pc + seg],
                                     q_ap, k_ap, start=st, stop=True,
                                     skip_group_check=True),
                            waits=wait0 if idx == 0 else [], inc=None)
                    assert masked
                    for mc_i, c in enumerate(masked):
                        jj = blocks.index(c)
                        mi = MASK_INDEX[(s, c)]
                        last = sch.emit(
                            "pe", lambda e, jj=jj, mi=mi, sbase=sbase:
                            e.matmul(self.ps_h[0:P, sbase + jj * P:
                                               sbase + (jj + 1) * P],
                                     self.ident[:],
                                     self.masks_sb[:, mi * P:(mi + 1) * P],
                                     start=False, stop=True,
                                     skip_group_check=True),
                            waits=buf("masks").wait_read(),
                            inc="spe" if mc_i == len(masked) - 1 else None)
                    t_sc = last
                    if not isinstance(getattr(self, "_pend", None), list):
                        self._pend = []
                    depth = 2 if g == 0 else 1
                    while len(self._pend) >= depth:
                        self._pend.pop(0)()
                    bn = buf("nsm")
                    t_b = sch.emit("dve", lambda e, L=L, ii=ii, sbase=sbase:
                                   e.reduce_max(
                                       self.nsm[:, ii % 2:ii % 2 + 1],
                                       self.ps_h[0:P, sbase:sbase + L * P],
                                       axis=AX.X, negate=True),
                                   waits=[t_sc] + bn.wait_write(),
                                   inc="sdve")
                    bn.set_written(t_b)
                    bp = buf("probs")
                    bl = buf("l_col")
                    t_es = []
                    for ch in range(nch):
                        n = min(8, L - ch * 8) * P
                        w = [t_b]
                        if ch == 0:
                            w += bp.wait_write() + bl.wait_write()
                        t_e = sch.emit("act", lambda e, ch=ch, n=n, ii=ii,
                                       sbase=sbase:
                                       e.activation(
                                           self.probs[:, ch * 1024:
                                                      ch * 1024 + n],
                                           self.ps_h[0:P, sbase + ch * 1024:
                                                     sbase + ch * 1024 + n],
                                           AF.Exp,
                                           bias=self.nsm[:, ii % 2:ii % 2 + 1],
                                           scale=1.0,
                                           accum_out=self.l_part[:, ch:ch + 1]),
                                       waits=w, inc="sact")
                        t_es.append(t_e)
                    for b_ in bsc:
                        b_.add_reader(t_es[-1])
                    bp.set_written(t_es[-1])
                    bn.add_reader(t_es[-1])
                    brc = buf("r_col")
                    if nch == 2:
                        t_l = sch.emit("dve", lambda e, ii=ii: e.tensor_tensor(
                            out=self.l_col[:, ii % 2:ii % 2 + 1],
                            in0=self.l_part[:, 0:1], in1=self.l_part[:, 1:2],
                            op=ALU.add), waits=[t_es[-1]], inc="sdve")
                        bl.set_written(t_l)
                        t_r = sch.emit("dve", lambda e, ii=ii: e.reciprocal(
                            self.r_col[:, ii % 2:ii % 2 + 1],
                            self.l_col[:, ii % 2:ii % 2 + 1]),
                            waits=[t_l] + brc.wait_write(), inc="sdve")
                        bl.add_reader(t_r)
                    else:
                        t_r = sch.emit("dve", lambda e, ii=ii: e.reciprocal(
                            self.r_col[:, ii % 2:ii % 2 + 1],
                            self.l_part[:, 0:1]),
                            waits=[t_es[-1]] + brc.wait_write(), inc="sdve")
                        bl.set_written(t_es[-1])
                        bl.add_reader(t_r)
                    brc.set_written(t_r)
                    pn_toks = []
                    for ch in range(nch):
                        nblk = min(8, L - ch * 8)
                        slot = (ii % 2) if nch == 1 else ch
                        bpn = buf(f"probs_n{slot}")
                        t_pn = sch.emit("dve", lambda e, ch=ch, nblk=nblk,
                                        slot=slot,
                                        ii=ii: e.tensor_scalar(
                                            out=self.probs_n[slot][:,
                                                              0:nblk * P],
                                            in0=self.probs[
                                                :, ch * 1024:
                                                ch * 1024 + nblk * P],
                                            scalar1=self.r_col[
                                                :, ii % 2:ii % 2 + 1],
                                            scalar2=None, op0=ALU.mult),
                                        waits=[t_r] + bpn.wait_write(),
                                        inc="sdve")
                        bpn.set_written(t_pn)
                        brc.add_reader(t_pn)
                        bp.add_reader(t_pn)
                        pn_toks.append(t_pn)
                    def deferred(qh=qh, L=L, nch=nch, ii=ii,
                                 pn_toks=tuple(pn_toks)):
                        for ch in range(nch):
                            nblk = min(8, L - ch * 8)
                            slot = (ii % 2) if nch == 1 else ch
                            bank = 4 + slot
                            pt = self.psb(bank, 1024)
                            bpb = buf(f"bank{bank}")
                            bpn = buf(f"probs_n{slot}")
                            lastt = None
                            for k in range(nblk):
                                lastt = sch.emit(
                                    "pe", lambda e, k=k, pt=pt, slot=slot:
                                    e.transpose(
                                        pt[:, k * P:(k + 1) * P],
                                        self.probs_n[slot][:,
                                                           k * P:(k + 1) * P],
                                        self.ident[:]),
                                    waits=([pn_toks[ch]] + bpb.wait_write())
                                    if k == 0 else [],
                                    inc="spe" if k == nblk - 1 else None)
                            bpb.set_written(lastt)
                            bpn.add_reader(lastt)
                            bpt = buf("probsT")
                            dst = self.probsT_all[:].rearrange(
                                "p (jj x) -> p jj x", x=512)[
                                :, ch * 8:ch * 8 + nblk,
                                qh * P:(qh + 1) * P]
                            ev = sch.emit("act", lambda e, dst=dst, pt=pt,
                                          nblk=nblk:
                                          e.copy(dst, pt[:, 0:nblk * P]
                                                 .rearrange(
                                                     "p (k x) -> p k x", x=P)),
                                          waits=[lastt] +
                                          (bpt.wait_write()
                                           if ch == 0 and qh == 0 else []),
                                          inc="sact")
                            bpb.add_reader(ev)
                            bpt.set_written(ev)
                    self._pend.append(deferred)
                while self._pend:
                    self._pend.pop(0)()
                av_bank = 6 + ((s * 8 + kvh) % 2)
                bav = buf(f"bank{av_bank}")
                w0 = bav.wait_write() + buf(f"v{g}").wait_read() + \
                    buf("probsT").wait_read()
                if g == 1:
                    w0 += buf("v0").wait_read()
                last = None
                first = True
                nb = len(blocks)
                for jj, c in enumerate(blocks):
                    v_ap = self.vb(self.O_KV + 32768 + c * 2048 + kvh * 256, P)
                    is_last = (jj == nb - 1)
                    last = sch.emit(
                        "pe", lambda e, jj=jj, v_ap=v_ap,
                        av_bank=av_bank, first=first, is_last=is_last:
                        e.matmul(
                            self.ps_h[0:P, av_bank * 512:
                                      av_bank * 512 + 512],
                            v_ap,
                            self.probsT_all[:, jj * 512:(jj + 1) * 512],
                            start=first, stop=is_last,
                            skip_group_check=True),
                        waits=w0 if first else [],
                        inc="spe" if is_last else None)
                    first = False
                t_av = last
                bav.set_written(t_av)
                buf("probsT").add_reader(t_av)
                dst = self.vb(self.O_R64, 32768).rearrange(
                    "p (hh j) -> p hh j", j=1024)[
                    :, kvh * 4:(kvh + 1) * 4, own_tok:own_tok + P]
                ev = sch.emit("dve", lambda e, dst=dst, av_bank=av_bank:
                              e.tensor_copy(
                                  dst,
                                  self.ps_h[0:P, av_bank * 512:
                                            av_bank * 512 + 512].rearrange(
                                      "p (q j) -> p q j", j=P)),
                              waits=[t_av], inc="sdve")
                bav.add_reader(ev)
                buf(f"attn_o{g}").set_written(ev)
                if bg is not None and (s * 8 + kvh) % 4 == 3:
                    next(bg, None)

    def dense(self):
        sch, buf = self.sch, self.buf
        for og in range(8):
            qwbs = []
            for q in range(2):
                wb = buf(f"dq{q}")
                tok = self.pool_dma(
                    self.dq[q],
                    self.wd_d[og][:, q * 8192:(q + 1) * 8192],
                    waits=wb.wait_write())
                wb.set_written(tok)
                qwbs.append(wb)
            pbs = [buf(f"bank{7 - b}") for b in range(8)]
            last_by_o = [None] * 8
            for fc in range(NFC):
                q = fc // 16
                wb = qwbs[q]
                wcol = (fc % 16) * 512
                for o in range(8):
                    w = []
                    if fc == 0:
                        w += pbs[o].wait_write()
                        w += buf("attn_o0").wait_read() + \
                            buf("attn_o1").wait_read()
                    if fc % 16 == 0:
                        w += wb.wait_read()
                    lhs = self.vb(self.O_R64 + fc * 2048 + o * 256, P)
                    last_by_o[o] = sch.emit(
                        "pe", lambda e, fc=fc, o=o, q=q, wcol=wcol, lhs=lhs:
                        e.matmul(self.psf(7 - o, 512), lhs,
                                 self.dq[q][:, wcol:wcol + 512],
                                 start=(fc == 0), stop=(fc == NFC - 1),
                                 skip_group_check=True),
                        waits=w,
                        inc="spe" if (fc == NFC - 1 or
                                      (fc % 16 == 15 and o == 7)) else None)
                if fc % 16 == 15:
                    wb.add_reader(last_by_o[7])
            xtoks = []
            for gi in range(2):
                bx = buf(f"xs2_{gi}")
                srcv = bass.AP(self.x_d.tensor,
                               gi * 1024 * D + og * 512,
                               [[D, P], [P * D, 4], [1, 512]])
                tokx = self.pool_dma(self.xs2[gi], srcv,
                                     waits=bx.wait_write())
                bx.set_written(tokx)
                xtoks.append((bx, tokx))
            for o in range(8):
                gi, oo = o // 4, o % 4
                bx, tokx = xtoks[gi]
                t_mm = sch.emit("pe", lambda e, o=o, gi=gi, oo=oo: e.matmul(
                    self.psf(7 - o, 512), self.ident[:],
                    self.xs2[gi][:, oo * 512:(oo + 1) * 512],
                    start=False, stop=True, skip_group_check=True),
                    waits=[last_by_o[o], tokx], inc="spe")
                bx.add_reader(t_mm)
                dst = self.vb(self.O_KV + o * 8192 + og * 1024, 512)
                t_ev = sch.emit("dve", lambda e, o=o, dst=dst:
                                e.tensor_copy(dst, self.psf(7 - o, 512)),
                                waits=[t_mm], inc="sdve")
                pbs[o].add_reader(t_ev)
                buf("x1").set_written(t_ev)
        if self.stop_after == "dense_mm":
            return
        sch.fence()
        pend = [None]
        for o in range(8):
            pb_ = o % 2
            x1v = self.vb(self.O_KV + o * 8192, 4096)
            em = sch.chain("dve")
            em(lambda e, x1v=x1v, pb_=pb_: e.tensor_scalar(
                out=self.xh2_tm[pb_], in0=x1v, scalar1=1.0, scalar2=None,
                op0=ALU.mult, op1=ALU.add, accum_out=self.s_col[:]),
                waits=buf(f"xh2_tm{pb_}").wait_write() +
                buf("stat").wait_write())
            em(lambda e, x1v=x1v, pb_=pb_: e.scalar_tensor_tensor(
                out=self.xh2_tm[pb_], in0=x1v, scalar=1.0, in1=x1v,
                op0=ALU.mult, op1=ALU.mult, accum_out=self.q_col[:]))
            em(lambda e: e.tensor_scalar(
                out=self.m_col[:], in0=self.s_col[:], scalar1=1.0 / D,
                scalar2=None, op0=ALU.mult))
            em(lambda e: e.tensor_tensor(
                out=self.s_col[:], in0=self.m_col[:], in1=self.m_col[:],
                op=ALU.mult))
            t_var = em(lambda e: e.scalar_tensor_tensor(
                out=self.var_col[:], in0=self.q_col[:], scalar=1.0 / D,
                in1=self.s_col[:], op0=ALU.mult, op1=ALU.subtract))
            t_ln = sch.emit("act", lambda e: e.activation(
                self.rstd_col[:], self.var_col[:], AF.Ln,
                bias=self.eps_col[:], scale=1.0),
                waits=[t_var] + buf("eps").wait_read(), inc="sact")
            t_rstd = sch.emit("act", lambda e: e.activation(
                self.rstd_col[:], self.rstd_col[:], AF.Exp, bias=0.0,
                scale=-0.5), waits=[t_ln], inc="sact")
            buf("stat").set_written(t_rstd)
            t_n = sch.emit("dve", lambda e, x1v=x1v, pb_=pb_: e.tensor_scalar(
                out=self.xh2_tm[pb_], in0=x1v, scalar1=self.m_col[:],
                scalar2=self.rstd_col[:], op0=ALU.subtract, op1=ALU.mult),
                waits=[t_rstd, t_var], inc="sdve")
            buf(f"xh2_tm{pb_}").set_written(t_n)
            buf("stat").add_reader(t_n)
            buf("x1").add_reader(t_n)
            if pend[0] is not None:
                pend[0]()

            def mk_pend(o=o, pb_=pb_, t_n=t_n):
                def run():
                    for bi in range(4):
                        bank = bi % 2
                        pt = self.psb(bank, 1024)
                        bpb = buf(f"bank{bank}")
                        last = None
                        for k in range(8):
                            col = (bi * 8 + k) * P
                            last = sch.emit(
                                "pe", lambda e, col=col, k=k, pt=pt, pb_=pb_:
                                e.transpose(
                                    pt[:, k * P:(k + 1) * P],
                                    self.xh2_tm[pb_][:, col:col + P],
                                    self.ident[:]),
                                waits=([t_n] + bpb.wait_write())
                                if k == 0 else [],
                                inc="spe" if k == 7 else None)
                        bpb.set_written(last)
                        buf(f"xh2_tm{pb_}").add_reader(last)
                        dst = self.vb(self.O_R64, 32768).rearrange(
                            "p (fc j) -> p fc j", j=1024)[
                            :, bi * 8:(bi + 1) * 8, o * P:(o + 1) * P]
                        src2 = pt[:].rearrange("p (k j) -> p k j", j=P)
                        ev = sch.emit("act", lambda e, dst=dst, src2=src2:
                                      e.copy(dst, src2), waits=[last],
                                      inc="sact")
                        bpb.add_reader(ev)
                        buf("xh2").set_written(ev)
                    self.ffn_passA(o)
                return run
            pend[0] = mk_pend()
        pend[0]()

    def ffn_consts(self):
        sch, buf = self.sch, self.buf
        w0 = buf("xh1_own").wait_write()
        for name, dst, src in (("wr_sb", self.wr_sb, self.wr_d),
                               ("lA_sb", self.lA_sb, self.lA_d),
                               ("lB_sb", self.lB_sb, self.lB_d),
                               ("bb_sb", self.bb_sb, self.bb_d),
                               ("br_sb", self.br_sb[:], self.br_d)):
            tok = self.pool_dma(dst, src, waits=w0)
            buf(name).set_written(tok)
        t_on = sch.emit("pool", lambda e: e.memset(self.ones_col[:], 1.0),
                        inc="spool")
        buf("ones_col").set_written(t_on)
        t_io = sch.emit("pool", lambda e: e.iota(
            self.kiota[:], pattern=[[1, 8]], base=1, channel_multiplier=0,
            allow_small_or_imprecise_dtypes=True),
            inc="spool")
        buf("kiota").set_written(t_io)

    def ffn_passA(self, o):
        sch, buf = self.sch, self.buf
        if True:
            rb_, hb_ = (2, 3) if o % 2 == 0 else (6, 7)
            b0, b1 = buf(f"bank{rb_}"), buf(f"bank{hb_}")
            lasth = None
            for fc in range(NFC):
                w = []
                if fc == 0:
                    w = (b0.wait_write() + b1.wait_write() +
                         buf("xh2").wait_read() + buf("wr_sb").wait_read() +
                         buf("lA_sb").wait_read())
                lhs = self.vb(self.O_R64 + fc * 2048 + o * 256, P)
                sch.emit("pe", lambda e, fc=fc, lhs=lhs, rb_=rb_: e.matmul(
                    self.psf(rb_, 8), lhs, self.wr_sb[:, fc * 8:(fc + 1) * 8],
                    start=(fc == 0), stop=False, skip_group_check=True),
                    waits=w)
                lasth = sch.emit("pe", lambda e, fc=fc, lhs=lhs, hb_=hb_:
                                 e.matmul(
                    self.psf(hb_, P), lhs, self.lA_sb[:, fc * P:(fc + 1) * P],
                    start=(fc == 0), stop=(fc == NFC - 1),
                    skip_group_check=True),
                    inc="spe" if fc == NFC - 1 else None)
            t_rb = sch.emit("pe", lambda e, rb_=rb_: e.matmul(
                self.psf(rb_, 8), self.ones_col[:], self.br_sb[:],
                start=False, stop=True, skip_group_check=True),
                waits=buf("br_sb").wait_read() + buf("ones_col").wait_read(),
                inc="spe")
            b0.set_written(t_rb)
            b1.set_written(lasth)
            em = sch.chain("dve")
            t_z = em(lambda e, rb_=rb_: e.tensor_copy(
                self.zbuf[:], self.psf(rb_, 8)),
                waits=[t_rb] + buf("spx").wait_write())
            b0.add_reader(t_z)
            em(lambda e: e.tensor_copy(self.zs0[:], self.zbuf[:]))
            cur, nxt = self.zs0, self.zs1
            stages = [
                [(0, 1), (2, 3), (4, 5), (6, 7)],
                [(0, 2), (1, 3), (4, 6), (5, 7)],
                [(1, 2), (5, 6)],
                [(0, 4), (1, 5), (2, 6), (3, 7)],
                [(2, 4), (3, 5)],
                [(1, 2), (3, 4), (5, 6)],
            ]
            for st in stages:
                covered = set()
                for (a, b) in st:
                    covered.add(a)
                    covered.add(b)
                    em(lambda e, a=a, b=b, cur=cur, nxt=nxt:
                       e.tensor_tensor(nxt[:, a:a + 1], cur[:, a:a + 1],
                                       cur[:, b:b + 1], op=ALU.max))
                    em(lambda e, a=a, b=b, cur=cur, nxt=nxt:
                       e.tensor_tensor(nxt[:, b:b + 1], cur[:, a:a + 1],
                                       cur[:, b:b + 1], op=ALU.min))
                for j in range(8):
                    if j not in covered:
                        em(lambda e, j=j, cur=cur, nxt=nxt:
                           e.tensor_copy(nxt[:, j:j + 1], cur[:, j:j + 1]))
                cur, nxt = nxt, cur
            zs = cur
            em(lambda e, zs=zs: e.tensor_copy(
                self.cums[:, 0:1], zs[:, 0:1]))
            em(lambda e, zs=zs: e.tensor_tensor(
                self.cums[:, 1:8], zs[:, 1:8], zs[:, 0:7], op=ALU.add))
            em(lambda e, nxt=nxt: e.tensor_copy(
                nxt[:, 0:2], self.cums[:, 0:2]))
            em(lambda e, nxt=nxt: e.tensor_tensor(
                nxt[:, 2:8], self.cums[:, 2:8], self.cums[:, 0:6], op=ALU.add))
            em(lambda e, nxt=nxt: e.tensor_copy(
                self.cums[:, 0:4], nxt[:, 0:4]))
            em(lambda e, nxt=nxt: e.tensor_tensor(
                self.cums[:, 4:8], nxt[:, 4:8], nxt[:, 0:4], op=ALU.add))
            em(lambda e, zs=zs, nxt=nxt: e.tensor_tensor(
                nxt[:], zs[:], self.kiota[:], op=ALU.mult),
                waits=buf("kiota").wait_read())
            em(lambda e: e.tensor_scalar(
                out=self.cums[:], in0=self.cums[:], scalar1=-1.0,
                scalar2=None, op0=ALU.add))
            em(lambda e, nxt=nxt: e.tensor_tensor(
                nxt[:], nxt[:], self.cums[:], op=ALU.is_gt))
            em(lambda e, nxt=nxt: e.reduce_sum(
                self.kz[:, 0:1], nxt[:], axis=AX.X))
            em(lambda e, zs=zs, nxt=nxt: e.scalar_tensor_tensor(
                out=zs[:], in0=nxt[:], scalar=1.0, in1=zs[:],
                op0=ALU.mult, op1=ALU.mult, accum_out=self.kz[:, 1:2]))
            em(lambda e: e.reciprocal(
                self.kz[:, 2:3], self.kz[:, 0:1]))
            em(lambda e: e.tensor_scalar(
                out=self.kz[:, 1:2], in0=self.kz[:, 1:2], scalar1=-1.0,
                scalar2=None, op0=ALU.add))
            em(lambda e: e.tensor_tensor(
                self.kz[:, 3:4], self.kz[:, 1:2], self.kz[:, 2:3],
                op=ALU.mult))
            t_w = em(lambda e: e.tensor_scalar(
                out=self.zbuf[:], in0=self.zbuf[:], scalar1=self.kz[:, 3:4],
                scalar2=0.0, op0=ALU.subtract, op1=ALU.max))
            buf("spx").set_written(t_w)
            bhw = buf("hw")
            t_hw = sch.emit("dve", lambda e, hb_=hb_: e.tensor_tensor(
                self.hw[:].rearrange("p (e r) -> p e r", r=16),
                self.psf(hb_, P).rearrange("p (e r) -> p e r", r=16),
                self.zbuf[:, :, None].broadcast_to([P, 8, 16]),
                op=ALU.mult),
                waits=[t_w, lasth] + bhw.wait_write(), inc="sdve")
            b1.add_reader(t_hw)
            bhw.set_written(t_hw)
            bpb = buf("bank5")
            pt = self.psb(5, 1024)
            t_tr = sch.emit("pe", lambda e, pt=pt: e.transpose(
                pt[:, 0:P], self.hw[:], self.ident[:]),
                waits=[t_hw] + bpb.wait_write(), inc="spe")
            bpb.set_written(t_tr)
            bhw.add_reader(t_tr)
            t_ht = sch.emit("dve", lambda e, o=o, pt=pt: e.tensor_copy(
                self.hwT[:, o * P:(o + 1) * P], pt[:, 0:P]),
                waits=[t_tr] + buf("h_wT").wait_write(), inc="sdve")
            bpb.add_reader(t_ht)
            buf("h_wT").set_written(t_ht)

    def ffn(self):
        sch, buf = self.sch, self.buf
        for og in range(8):
            qwbs = []
            for q in range(2):
                wb = buf(f"dq{q}")
                tok = self.pool_dma(
                    self.dq[q],
                    self.wb_d[og][:, q * 8192:(q + 1) * 8192],
                    waits=wb.wait_write())
                wb.set_written(tok)
                qwbs.append(wb)
            pbs = [buf(f"bank{b}") for b in range(8)]
            last_by_o = [None] * 8
            for fc in range(NFC):
                q = fc // 16
                wb = qwbs[q]
                wcol = (fc % 16) * 512
                for o in range(8):
                    w = []
                    if fc == 0:
                        w += pbs[o].wait_write() + buf("xh2").wait_read()
                    if fc % 16 == 0:
                        w += wb.wait_read()
                    lhs = self.vb(self.O_R64 + fc * 2048 + o * 256, P)
                    last_by_o[o] = sch.emit(
                        "pe", lambda e, fc=fc, o=o, q=q, wcol=wcol, lhs=lhs:
                        e.matmul(self.psf(o, 512), lhs,
                                 self.dq[q][:, wcol:wcol + 512],
                                 start=(fc == 0), stop=False,
                                 skip_group_check=True),
                        waits=w,
                        inc="spe" if (fc == NFC - 1 or
                                      (fc % 16 == 15 and o == 7)) else None)
                if fc % 16 == 15:
                    wb.add_reader(last_by_o[7])
            for o in range(8):
                sch.emit("pe", lambda e, o=o, og=og: e.matmul(
                    self.psf(o, 512),
                    self.hwT[:, o * P:(o + 1) * P],
                    self.lB_sb[:, og * 512:(og + 1) * 512],
                    start=False, stop=False, skip_group_check=True),
                    waits=buf("h_wT").wait_read() + buf("lB_sb").wait_read())
                last = sch.emit("pe", lambda e, o=o, og=og: e.matmul(
                    self.psf(o, 512), self.ones_col[:],
                    self.bb_sb[0:1, og * 512:(og + 1) * 512],
                    start=False, stop=True, skip_group_check=True),
                    waits=buf("bb_sb").wait_read(), inc="spe")
                x1v = self.vb(self.O_KV + o * 8192 + og * 1024, 512)
                last = sch.emit("pe", lambda e, o=o, x1v=x1v: e.matmul(
                    self.psf(o, 512), self.ident[:], x1v,
                    start=False, stop=True, skip_group_check=True),
                    waits=[last] + buf("x1").wait_read(), inc="spe")
                pbs[o].set_written(last)
                bo = buf(f"op{o % 2}")
                t_ev = sch.emit("dve", lambda e, o=o:
                                e.tensor_copy(self.op_[o % 2],
                                              self.psf(o, 512)),
                                waits=[last] + bo.wait_write(), inc="sdve")
                pbs[o].add_reader(t_ev)
                bo.set_written(t_ev)
                t_st = sch.emit("sync", lambda e, o=o, og=og: e.dma_start(
                    out=self.y_d[o * P:(o + 1) * P, og * 512:(og + 1) * 512],
                    in_=self.op_[o % 2]),
                    waits=[t_ev], inc="sst", amt=16)
                bo.add_reader(t_st)


def build_masks(p):
    m = np.zeros((NMASK, P, P), dtype=np.float32)
    tri = np.triu(np.full((P, P), NEG, dtype=np.float32), k=1)
    for cls, mi in _PAIRS.items():
        c = cls[p]
        if c == "tri":
            m[mi] = tri
        elif c == "neg":
            m[mi] = NEG
    return np.ascontiguousarray(
        m.transpose(1, 0, 2).reshape(P, NMASK * P)).astype(ml_dtypes.bfloat16)


def _pack_weights(inputs):
    d = {}
    d["wqkv_p"] = np.ascontiguousarray(
        np.asarray(inputs["L0_attn_w_qkv"], dtype=np.float32)
        .reshape(48, 128, 32, 128).transpose(0, 3, 2, 1).reshape(48, 128, 4096))
    d["wd_p"] = np.ascontiguousarray(
        np.asarray(inputs["L0_attn_w_dense"], dtype=np.float32)
        .reshape(8, 512, 32, 128).transpose(0, 3, 2, 1).reshape(8, 128, 16384))
    d["wb_p"] = np.ascontiguousarray(
        np.asarray(inputs["L0_ffn_W_base"], dtype=np.float32)
        .reshape(8, 512, 32, 128).transpose(0, 3, 2, 1).reshape(8, 128, 16384))
    d["wr_p"] = np.ascontiguousarray(
        np.asarray(inputs["L0_ffn_W_router"], dtype=np.float32)
        .T.reshape(32, 128, 8).transpose(1, 0, 2).reshape(128, 256))
    d["lA_p"] = np.ascontiguousarray(
        np.asarray(inputs["L0_ffn_lora_A"], dtype=np.float32)
        .transpose(1, 0, 2).reshape(32, 128, 128).transpose(1, 0, 2)
        .reshape(128, 4096))
    d["lB_p"] = np.ascontiguousarray(
        np.asarray(inputs["L0_ffn_lora_B"], dtype=np.float32).reshape(128, 4096))
    d["bb"] = np.ascontiguousarray(
        np.asarray(inputs["L0_ffn_b_base"], dtype=np.float32).reshape(1, D))
    d["br"] = np.ascontiguousarray(
        np.asarray(inputs["L0_ffn_b_router"], dtype=np.float32).reshape(1, 8))
    return d


def kernel(**inputs):
    x = np.asarray(inputs["L0_attn_q"], dtype=np.float32)
    shared = _pack_weights(inputs)
    kb = KB()
    nc = kb.build()
    in_maps = []
    for c in range(8):
        b, p = c // 2, c % 2
        perm = np.concatenate(
            [np.arange(t * P, (t + 1) * P) for t in col_order(p)])
        m = dict(shared)
        m["x_perm"] = np.ascontiguousarray(x[b][perm])
        m["masks"] = build_masks(p)
        in_maps.append(m)
    res = run_bass_kernel_spmd(nc, in_maps, core_ids=list(range(8)))
    y = np.empty((B, S, D), dtype=np.float32)
    for c in range(8):
        b, p = c // 2, c % 2
        out = res.results[c]["y_out"]
        for o, t in enumerate(T_SETS[p]):
            y[b, t * P:(t + 1) * P, :] = out[o * P:(o + 1) * P, :]
    return y
